# version 04
# baseline (speedup 1.0000x reference)
import contextlib
import numpy as np

import concourse.bass as bass
import concourse.mybir as mybir
from concourse.alu_op_type import AluOpType as OP
from concourse.bass_utils import run_bass_kernel_spmd

F32 = mybir.dt.float32
BF16 = mybir.dt.bfloat16
U32 = mybir.dt.uint32
AF = mybir.ActivationFunctionType

H, D, CH, CPS = 4, 1024, 32, 8
SCT = CH * CPS
HALO = 128
E, FF = 8, 4096

ENG = {"pe": "tensor", "dve": "vector", "act": "scalar", "pool": "gpsimd", "sp": "sync"}


class Bf:
    __slots__ = ("t", "w", "rs")

    def __init__(self, t):
        self.t = t
        self.w = None
        self.rs = []


class PB:
    def __init__(self, nc, es, names):
        self.nc = nc
        self.sem = {n: es.enter_context(nc.semaphore(n)) for n in names}
        self.cnt = {n: 0 for n in names}
        self.hw = {}

    def _w(self, eng, ek, tok):
        if tok is None:
            return
        s, v = tok
        if self.hw.get((ek, s), 0) < v:
            eng.wait_ge(self.sem[s], v)
            self.hw[(ek, s)] = v

    def pre(self, eng, ek, reads=(), writes=(), toks=(), skip_sem=None):
        for b in reads:
            if b.w is not None and b.w[0] != skip_sem:
                self._w(eng, ek, b.w)
        for b in writes:
            if b.w is not None and b.w[0] != skip_sem:
                self._w(eng, ek, b.w)
            for r in b.rs:
                self._w(eng, ek, r)
        for t in toks:
            self._w(eng, ek, t)

    def post(self, tok, reads=(), writes=()):
        for b in writes:
            b.w = tok
            b.rs = []
        for b in reads:
            b.rs.append(tok)
        return tok

    def fin(self, inst, s):
        c = self.cnt[s]
        inst.then_inc(self.sem[s], 1)
        if c > 0:
            inst._wait_ge(self.sem[s], c)
        self.cnt[s] = c + 1
        return (s, c + 1)

    def dmatok(self, inst, s):
        inst.then_inc(self.sem[s], 16)
        self.cnt[s] += 16
        return (s, self.cnt[s])

    def endburst(self, qs, *bufs):
        t = (qs, self.cnt[qs])
        for b in bufs:
            b.rs = [r for r in b.rs if r[0] != qs] + [t]
            if b.w is not None and b.w[0] == qs:
                b.w = t


def ts(i, n):
    return slice(i * n, (i + 1) * n)


def build_program(NSC=8, taps=(), pairs=4, stop_after=None):
    tful_holder = [None]
    NT = NSC * SCT
    NTH = NT + HALO
    NG = NT // 512
    nc = bass.Bass("TRN2", target_bir_lowering=False, debug=False)
    tapset = set(taps)

    dt = lambda n, s, d=F32, k="ExternalInput": nc.dram_tensor(n, s, d, kind=k)
    x_h = dt("x", [NTH, D])
    w_h = {n: dt(n, [D, D]) for n in ("wq", "wk", "wv", "wo")}
    wbid_h = dt("wbid", [D, 256])
    bidp_h = dt("bidp", [128, 1]); alphap_h = dt("alphap", [128, 1])
    conv_h = {n: dt("conv" + n, [128, 32]) for n in "qkv"}
    firs_h = dt("firs", [128, 24]); firl_h = dt("firl", [128, 248])
    wr1m_h = dt("wr1m", [D, 2048]); wr1sm_h = dt("wr1sm", [12, 2048]); wr1ss_h = dt("wr1ss", [12, 2048])
    br1_h = dt("br1r", [128, 16]); wr2_h = dt("wr2", [2048, 12]); br2_h = dt("br2r", [12, 1])
    ltg12_h = dt("ltg12", [12, 1])
    onw_h = dt("onwr", [128, 2])
    cen_h = dt("cent", [D, E])
    lng_h = dt("lng", [E, D]); lnb_h = dt("lnb", [E, D])
    w1_h = dt("w1", [E, D, FF]); w2_h = dt("w2", [E, FF, D])
    b1_h = dt("b1r", [128, E * 32]); b2p_h = dt("b2p", [E * 32, D])
    i32_h = dt("i32t8", [128, 256])
    i32sp_h = dt("i32sp", [128, 256])
    ones_h = dt("oneskm", [128, 128]); m1_h = dt("m1km", [128, 128])
    colsel_h = dt("colsel", [128, 64])
    statsel_h = dt("statsel", [128, 288])
    g2m_h = dt("g2m", [128, CPS * 256])
    nmask_h = dt("nmask", [128, 256])
    gatesel_h = dt("gatesel", [128, 12 * 128])
    rsel_h = dt("rsel", [128, 512])
    rselm_h = dt("rselm", [128, 512])
    rself_h = dt("rself", [128, 512])
    hflag_h = dt("hflag", [128, 1])
    hflagn_h = dt("hflagn", [128, 1])
    epsv_h = dt("epsv", [128, 4])
    out_h = dt("out", [NT, D], F32, "ExternalOutput")
    TAPS = {}
    tapdefs = {"q": [D, NT], "k": [D, NT], "v": [D, NT], "kb": [D, NT],
               "beta": [128, NT], "idsc": [128, NT], "xn": [NTH, D],
               "delta": [NT, D], "send": [128, 2048], "fs": [D, NT], "fl": [D, NT],
               "smean": [12, NT], "sstd": [12, NT], "p": [16, NT], "o": [D, NT],
               "h1": [NT, D], "xin": [NT, D], "eid": [NT // 128, 128, 2],
               "xln": [NT, D], "hmid": [2048, NT],
               "g1t": [128, CPS * 256], "g2t": [128, CPS * 256], "n1t": [128, 256],
               "tful": [128, 256], "u2": [128, CPS * 256], "u2p": [128, 2048],
               "zt": [128, CPS * 256]}
    for tn in tapset:
        TAPS[tn] = dt("tap_" + tn, tapdefs[tn], F32, "ExternalOutput")

    scr_xn = dt("scr_xn", [NTH, D], BF16, "Internal")
    scr_k = dt("scr_k", [D, NT], BF16, "Internal")
    scr_v = dt("scr_v", [D, NT], BF16, "Internal")
    scr_d = dt("scr_d", [NT, D], BF16, "Internal")
    scr_xin = dt("scr_xin", [NT, D], BF16, "Internal")
    scr_h1 = dt("scr_h1", [NT, D], BF16, "Internal")
    scr_xln = dt("scr_xln", [NT, D], BF16, "Internal")
    scr_u2 = dt("scr_u2", [4, 256, 256], BF16, "Internal")
    scr_lg = dt("scr_lg", [16, 256], BF16, "Internal")
    scr_p = dt("scr_p", [256, 128], BF16, "Internal")
    cc_i = nc.dram_tensor("ccs_in", [1024, 256], F32)
    cc_o = nc.dram_tensor("ccs_out", [1024, 256], F32)

    es = contextlib.ExitStack()
    with es:
        ats = contextlib.ExitStack()

        def sb(n, s, d=BF16):
            return Bf(es.enter_context(nc.sbuf_tensor(n, s, d)))

        def sbA(n, s, d=BF16):
            return Bf(ats.enter_context(nc.sbuf_tensor(n, s, d)))

        def psum(n, s):
            return Bf(es.enter_context(nc.psum_tensor(n, s, F32)))

        epsv = sb("epsv_s", [128, 4], F32)
        oneskm = sb("ones_s", [128, 128])
        mv = sb("mv_s", [128, 16], F32)
        sc1 = sb("sc1_s", [128, 1024], F32)
        h1t = [sb("h1t0", [128, D])]
        wpb = [sbA(f"wpb{i}_s", [128, D]) for i in range(2)]
        wbid = sbA("wbid_s", [128, 8 * 256])
        wr1 = sbA("wr1_s", [128, 8 * 2048])
        wr1sm = sbA("wr1sm_s", [12, 2048]); wr1ss = sbA("wr1ss_s", [12, 2048])
        wr2 = sbA("wr2_s", [128, 16 * 12])
        wob = [sbA(f"wob{i}_s", [128, D]) for i in range(2)]
        convw = sbA("convw_s", [128, 96], F32)
        firw = sbA("firw_s", [128, 272], F32)
        br1 = sbA("br1_s", [128, 16], F32); br2 = sbA("br2_s", [12, 1], F32)
        tau12 = sbA("tau12_s", [12, 1], F32); it12 = sbA("it12_s", [12, 1], F32)
        ltg = sbA("ltg12_s", [12, 1], F32)
        onw = sbA("onw_s", [128, 2], F32)
        bidp = sbA("bidp_s", [128, 1], F32); alphap = sbA("alphap_s", [128, 1], F32)
        siga = sbA("siga_s", [128, 1], F32)
        hflag = sbA("hflag_s", [128, 1], F32)
        hflagn = sbA("hflagn_s", [128, 1], F32)
        i32t = sbA("i32t_s", [128, 256])
        i32sp = sbA("i32sp_s", [128, 256])
        oneskmf = sbA("onesf_s", [128, 128], F32)
        m1km = sbA("m1_s", [128, 128])
        colsel = sbA("colsel_s", [128, 64])
        statsel = sbA("statsel_s", [128, 288])
        g2m = sbA("g2m_s", [128, CPS * 256])
        nmask = sbA("nmask_s", [128, 256])
        gatesel = sbA("gatesel_s", [128, 12 * 128])
        rsel = sbA("rsel_s", [128, 512])
        rselm = sbA("rselm_s", [128, 512])
        rself = sbA("rself_s", [128, 512], F32)
        xnf = sbA("xnf_s", [128, 8 * 384])
        praw = sbA("praw_s", [128, 320])
        qf = sbA("qf_s", [128, 8 * 256]); kf = sbA("kf_s", [128, 8 * 256])
        vf = sbA("vf_s", [128, 8 * 288])
        kbn = sbA("kbn_s", [128, 8 * 256])
        sq = sbA("sq_s", [128, 8 * 256])
        betap = sbA("betap_s", [128, 256])
        idsc = sbA("idsc_s", [128, 256])
        nrm = sbA("nrm_s", [128, 512], F32)
        fac = sbA("fac_s", [128, 320], F32)
        fac2 = sbA("fac2_s", [128, 256], F32)
        fsf = sbA("fsf_s", [128, 8 * 256]); flf = sbA("flf_s", [128, 8 * 256])
        g1t = sbA("g1t_s", [128, CPS * 256]); g2t = sbA("g2t_s", [128, CPS * 256])
        nst = {k: sbA(f"n{k}_s", [128, 256]) for k in ["1t", "1", "2", "2t", "4", "4t", "8", "8t", "16", "16t"]}
        rt = [sbA(f"rt{i}_s", [128, 256]) for i in range(3)]
        zsb = sbA("zsb_s", [128, 256])
        u2il = sbA("u2il_s", [128, CPS * 256])
        u2pl = sbA("u2pl_s", [128, 2 * 1024])
        vtm = sbA("vtm_s", [128, 2 * 1024])
        ktm = sbA("ktm_s", [128, 2 * 1024])
        sfp = sbA("sfp_s", [128, 2048], F32)
        sbfb = sbA("sbf_s", [128, 2048])
        dfm = sbA("dfm_s", [128, 8 * 256])
        smeanb = sbA("smeanb_s", [12, 256]); sstdb = sbA("sstdb_s", [12, 256])
        hmid = sbA("hmid_s", [128, 8 * 256])
        lgp = sbA("lgp_s", [16, 256])
        ph_ = sbA("ph_s", [128, 32])
        ptf = sbA("ptf_s", [128, 16], F32)
        mxg = sbA("mxg_s", [128, 4], F32)
        dif = sbA("dif_s", [128, 16], F32)
        exg = sbA("exg_s", [128, 16], F32)
        sumr = sbA("sumr_s", [128, 4], F32)
        ptmb = sbA("ptmb_s", [128, 128])
        pfm = sbA("pfm_s", [128, 256])
        of32 = sbA("of32_s", [128, 512], F32)
        gbh = sbA("gbh_s", [128, 1024])
        tmx = [sbA("tmx0_s", [128, 256], F32)]
        off = sbA("off_s", [128, 8 * 256])
        xres = [sbA("xres0", [128, D], F32)]
        xint = [sbA("xint0", [128, D])]

        PA = psum("PA", [128, 512]); PBp = psum("PB", [128, 512])
        PC = psum("PC", [128, 512]); PD = psum("PD", [128, 512])
        PG1 = psum("PG1", [128, 1024]); PG2 = psum("PG2", [128, 1024])

        Dxn = Bf(scr_xn); Dk = Bf(scr_k); Dv = Bf(scr_v); Dd = Bf(scr_d)
        Dxin = Bf(scr_xin); Dh1 = Bf(scr_h1); Dxln = Bf(scr_xln)
        Dlg = Bf(scr_lg); Dp = Bf(scr_p); Dci = Bf(cc_i); Dco = Bf(cc_o)
        Du2 = Bf(scr_u2)

        p = PB(nc, es, ["pe", "dve", "act", "pool", "qw", "qx", "qxn", "qkv", "qd", "qhx", "qlg", "qg", "qo", "cc", "qp0", "qp1", "qb0", "qb1", "qA", "qB", "qC", "qD", "hxnf", "htm", "hd", "hlg", "hp", "hxf", "qhxb", "qu2", "qud"])
        blk = es.enter_context(nc.Block())
        SEC = lambda ek, fn: getattr(blk, ENG[ek])(fn)

        def DMA(g, qs, outb, outap, inb, inap, ek="pool", **kw):
            wr = [outb] if outb is not None else []
            rd = [inb] if inb is not None else []
            p.pre(g, ek, reads=rd, writes=wr, skip_sem=qs)
            tok = p.dmatok(g.dma_start(out=outap, in_=inap, **kw), qs)
            if qs == "qo":
                p._w(g, ek, tok)
            p.post(tok, reads=rd, writes=wr)
            return tok

        def TDMA(s, qs, outb, outap, inb, inap, ek="sp"):
            wr = [outb] if outb is not None else []
            rd = [inb] if inb is not None else []
            p.pre(s, ek, reads=rd, writes=wr, skip_sem=qs)
            tok = p.dmatok(s.dma_start(out=outap, in_=inap, transpose=True), qs)
            p.post(tok, reads=rd, writes=wr)
            return tok

        def ph0(g):
            ld = lambda o_ap, i_ap: p.dmatok(g.dma_start(out=o_ap, in_=i_ap), "qw")
            for kc in range(8):
                ld(wr1.t[:, ts(kc, 2048)], wr1m_h[ts(kc, 128), :])
                ld(wbid.t[:, ts(kc, 256)], wbid_h[ts(kc, 128), :])
            for kc in range(16):
                ld(wr2.t[:, ts(kc, 12)], wr2_h[ts(kc, 128), :])
            ld(wr1sm.t[:, :], wr1sm_h[:, :]); ld(wr1ss.t[:, :], wr1ss_h[:, :])
            ld(convw.t[:, 0:32], conv_h["q"][:, :]); ld(convw.t[:, 32:64], conv_h["k"][:, :])
            ld(convw.t[:, 64:96], conv_h["v"][:, :])
            ld(firw.t[:, 0:24], firs_h[:, :]); ld(firw.t[:, 24:272], firl_h[:, :])
            ld(br1.t[:, :], br1_h[:, :]); ld(br2.t[:, :], br2_h[:, :])
            ld(ltg.t[:, :], ltg12_h[:, :]); ld(onw.t[:, :], onw_h[:, :])
            ld(bidp.t[:, :], bidp_h[:, :]); ld(alphap.t[:, :], alphap_h[:, :])
            ld(hflag.t[:, :], hflag_h[:, :]); ld(hflagn.t[:, :], hflagn_h[:, :])
            ld(i32t.t[:, :], i32_h[:, :])
            ld(epsv.t[:, :], epsv_h[:, :])
            ld(i32sp.t[:, :], i32sp_h[:, :])
            ld(oneskm.t[:, :], ones_h[:, :]); ld(oneskmf.t[:, :], ones_h[:, :])
            ld(m1km.t[:, :], m1_h[:, :]); ld(colsel.t[:, :], colsel_h[:, :])
            ld(statsel.t[:, :], statsel_h[:, :]); ld(g2m.t[:, :], g2m_h[:, :])
            ld(nmask.t[:, :], nmask_h[:, :]); ld(gatesel.t[:, :], gatesel_h[:, :])
            ld(rsel.t[:, :], rsel_h[:, :]); ld(rselm.t[:, :], rselm_h[:, :])
            ld(rself.t[:, :], rself_h[:, :])
            tok = ("qw", p.cnt["qw"])
            for b in [wbid, wr1, wr1sm, wr1ss, wr2, convw, firw, br1,
                      br2, ltg, onw, bidp, alphap, hflag, i32t, oneskm, oneskmf, m1km,
                      colsel, statsel, g2m, nmask, gatesel, rsel, rselm, rself, epsv, hflagn, i32sp]:
                b.w = tok
        SEC("pool", ph0)

        def ph0b(a):
            p.pre(a, "act", reads=[alphap, ltg], writes=[siga, tau12])
            t1 = p.fin(a.activation(siga.t[:, :], alphap.t[:, :], AF.Sigmoid), "act")
            p.post(t1, reads=[alphap], writes=[siga])
            t2 = p.fin(a.activation(tau12.t[:, :], ltg.t[:, :], AF.Exp), "act")
            p.post(t2, reads=[ltg], writes=[tau12])
        SEC("act", ph0b)

        def ph0c(v):
            p.pre(v, "dve", reads=[tau12], writes=[it12, sfp, sbfb, lgp, ptmb])
            t = p.fin(v.reciprocal(it12.t[:, :], tau12.t[:, :]), "dve")
            p.post(t, reads=[tau12], writes=[it12])
            p.post(p.fin(v.memset(sfp.t[:, :], 0.0), "dve"), writes=[sfp])
            p.post(p.fin(v.memset(sbfb.t[:, :], 0.0), "dve"), writes=[sbfb])
            p.post(p.fin(v.memset(lgp.t[:, :], 0.0), "dve"), writes=[lgp])
            p.post(p.fin(v.memset(ptmb.t[:, :], 0.0), "dve"), writes=[ptmb])
        SEC("dve", ph0c)

        xt = [xres[0]]
        xnt = [h1t[0]]
        ntile = NTH // 128
        for i in range(ntile):
            b = 0

            def f1(s, i=i, b=b):
                DMA(s, "qx", xt[b], xt[b].t[:, :], None, x_h[ts(i, 128), :], ek="sp")
            SEC("sp", f1)

            def f2(v, i=i, b=b):
                p.pre(v, "dve", reads=[xt[b]], writes=[mv])
                p.fin(v.bn_stats(mv.t[:, 0:6], xt[b].t[:, 0:512]), "dve")
                p.fin(v.bn_stats(mv.t[:, 6:12], xt[b].t[:, 512:1024]), "dve")
                t = p.fin(v.bn_aggr(mv.t[:, 12:14], mv.t[:, 0:12]), "dve")
                p.post(t, reads=[xt[b]], writes=[mv])
            SEC("dve", f2)

            def f3(a, i=i):
                p.pre(a, "act", reads=[mv], writes=[mv])
                p.fin(a.activation(mv.t[:, 14:15], mv.t[:, 13:14], AF.Ln, bias=epsv.t[:, 0:1]), "act")
                t = p.fin(a.activation(mv.t[:, 15:16], mv.t[:, 14:15], AF.Exp, scale=epsv.t[:, 2:3]), "act")
                p.post(t, writes=[mv])
            SEC("act", f3)

            def f4(v, i=i, b=b):
                p.pre(v, "dve", reads=[xt[b], mv], writes=[xnt[b]])
                t = p.fin(v.tensor_scalar(xnt[b].t[:, :], xt[b].t[:, :], mv.t[:, 12:13],
                                          mv.t[:, 15:16], op0=OP.subtract, op1=OP.mult), "dve")
                p.post(t, reads=[xt[b], mv], writes=[xnt[b]])
            SEC("dve", f4)

            def f5(g, i=i, b=b):
                DMA(g, "qxn", Dxn, scr_xn[ts(i, 128), :], xnt[b], xnt[b].t[:, :])
                if "xn" in TAPS:
                    DMA(g, "qo", None, TAPS["xn"][ts(i, 128), :], xnt[b], xnt[b].t[:, :])
            SEC("pool", f5)

        def early_out():
            snapx = dict(p.cnt)

            def _fx(e, ek):
                for s_, v_ in snapx.items():
                    if v_ > 0:
                        p._w(e, ek, (s_, v_))
            for ek_ in ("pe", "dve", "act", "pool", "sp"):
                SEC(ek_, lambda e, ek_=ek_: _fx(e, ek_))
            ats.close()

            def eo(v):
                p.pre(v, "dve", writes=[sc1])
                t = p.fin(v.memset(sc1.t[:, :], 0.0), "dve")
                p.post(t, writes=[sc1])
            SEC("dve", eo)

            def eo2(g):
                for i in range(NT // 128):
                    DMA(g, "qo", None, out_h[ts(i, 128), :], sc1, sc1.t[:, :])
                g.wait_ge(p.sem["qo"], p.cnt["qo"])
            SEC("pool", eo2)

        if stop_after == "p1":
            early_out()
            return nc

        snap1 = dict(p.cnt)

        def _fence1(e, ek):
            for s_, v_ in snap1.items():
                if v_ > 0:
                    p._w(e, ek, (s_, v_))
        for ek_ in ("pe", "dve", "act", "pool", "sp"):
            SEC(ek_, lambda e, ek_=ek_: _fence1(e, ek_))

        def sctk(sc):
            return slice(sc * SCT, (sc + 1) * SCT)

        def emit_sc(sc, full, upto=None):
            def a_hop(s):
                for pc in range(8):
                    TDMA(s, "hxnf", xnf if pc == 7 else None, xnf.t[:, ts(pc, 384)],
                         Dxn, scr_xn[sc * SCT: sc * SCT + 384, ts(pc, 128)])
            SEC("sp", a_hop)
            p.endburst("hxnf", Dxn)
            xnf.w = ("hxnf", p.cnt["hxnf"])
            xnf.rs = []

            projs = [("q", "wq", 0, qf, 256, 61), ("k", "wk", 32, kf, 256, 61),
                     ("v", "wv", 64, vf, 288, 29)]
            if not full:
                projs = projs[1:]
            PSUM8 = [(PA, slice(0, 320)), (PBp, slice(0, 320)), (PC, slice(0, 320)),
                     (PD, slice(0, 320)), (PG1, slice(0, 320)), (PG1, slice(512, 832)),
                     (PG2, slice(0, 320)), (PG2, slice(512, 832))]
            for pn, wname, cwo, dstf, wide, base in projs:
                for kc in range(8):
                    wb = wpb[kc % 2]

                    def fwl(g, wname=wname, kc=kc, wb=wb):
                        DMA(g, ("qp0" if kc % 2 == 0 else "qp1"), wb, wb.t[:, :], None, w_h[wname][ts(kc, 128), :])
                    SEC("pool", fwl)

                    def fmm(t, kc=kc, wb=wb):
                        wrs_ = [PA, PBp, PC, PD, PG1, PG2] if kc == 0 else []
                        p.pre(t, "pe", reads=[xnf, wb], writes=wrs_)
                        for pc in range(8):
                            PT, reg = PSUM8[pc]
                            inst = t.matmul(PT.t[:, reg], wb.t[:, ts(pc, 128)],
                                            xnf.t[:, kc * 384 + 64: (kc + 1) * 384],
                                            start=(kc == 0), stop=(kc == 7))
                        tk = p.fin(inst, "pe")
                        p.post(tk, reads=[xnf, wb],
                               writes=[PA, PBp, PC, PD, PG1, PG2] if kc == 7 else [])
                    SEC("pe", fmm)

                for pc in range(8):
                    PT, reg = PSUM8[pc]

                    def fev(a, PT=PT, reg=reg):
                        p.pre(a, "act", reads=[PT], writes=[praw])
                        t = p.fin(a.copy(praw.t[:, :], PT.t[:, reg]), "act")
                        p.post(t, reads=[PT], writes=[praw])
                    SEC("act", fev)

                    def fcv(v, pn=pn, pc=pc, cwo=cwo, wide=wide, base=base):
                        p.pre(v, "dve", reads=[praw, convw], writes=[fac])
                        p.fin(v.tensor_scalar_mul(fac.t[:, 0:wide], praw.t[:, base: base + wide],
                                                  convw.t[:, cwo + pc * 4: cwo + pc * 4 + 1]), "dve")
                        for j in range(1, 4):
                            t = p.fin(v.scalar_tensor_tensor(
                                fac.t[:, 0:wide], praw.t[:, base + j: base + j + wide],
                                convw.t[:, cwo + pc * 4 + j: cwo + pc * 4 + j + 1],
                                fac.t[:, 0:wide], op0=OP.mult, op1=OP.add), "dve")
                        p.post(t, reads=[praw, convw], writes=[fac])
                    SEC("dve", fcv)

                    def fsl(a, pc=pc, dstf=dstf, wide=wide):
                        p.pre(a, "act", reads=[fac], writes=[dstf])
                        t = p.fin(a.activation(dstf.t[:, pc * wide: (pc + 1) * wide],
                                               fac.t[:, 0:wide], AF.Silu), "act")
                        p.post(t, reads=[fac], writes=[dstf])
                    SEC("act", fsl)

            if upto == "proj":
                return
            def cmm(t):
                p.pre(t, "pe", reads=[xnf, wbid], writes=[PC])
                for msl in range(2 if full else 1):
                    for kc in range(8):
                        inst = t.matmul(PC.t[:, msl * 256: msl * 256 + 256],
                                        wbid.t[:, kc * 256 + msl * 128: kc * 256 + msl * 128 + 128],
                                        xnf.t[:, kc * 384 + 128: (kc + 1) * 384],
                                        start=(kc == 0), stop=(kc == 7))
                p.post(p.fin(inst, "pe"), reads=[xnf, wbid], writes=[PC])
            SEC("pe", cmm)

            def cact(a):
                p.pre(a, "act", reads=[PC, bidp], writes=[betap, idsc])
                t = p.fin(a.activation(betap.t[:, 0:256], PC.t[:, 0:256], AF.Sigmoid), "act")
                if full:
                    t = p.fin(a.activation(idsc.t[:, 0:256], PC.t[:, 256:512], AF.Sigmoid,
                                           bias=bidp.t[:, 0:1]), "act")
                p.post(t, reads=[PC, bidp], writes=[betap] + ([idsc] if full else []))
            SEC("act", cact)

            if full:
                def cid(v):
                    p.pre(v, "dve", reads=[siga], writes=[idsc])
                    t = p.fin(v.tensor_scalar(idsc.t[:, 0:256], idsc.t[:, 0:256],
                                              siga.t[:, 0:1], 0.06, op0=OP.mult, op1=OP.add), "dve")
                    p.post(t, reads=[siga], writes=[idsc])
                SEC("dve", cid)

            if upto == "beta":
                return
            nlist = [("q", qf, 0, PC), ("k", kf, 256, PD)] if full else [("k", kf, 256, PD)]
            for pn, dstf, nro, PT in nlist:
                def dsq(v, dstf=dstf):
                    p.pre(v, "dve", reads=[dstf], writes=[sq])
                    for pc in range(8):
                        t = p.fin(v.tensor_mul(sq.t[:, ts(pc, 256)], dstf.t[:, ts(pc, 256)],
                                               dstf.t[:, ts(pc, 256)]), "dve")
                    p.post(t, reads=[dstf], writes=[sq])
                SEC("dve", dsq)

                def dmm(t, PT=PT):
                    p.pre(t, "pe", reads=[sq, colsel], writes=[PT])
                    for h in range(H):
                        for kci in range(2):
                            pc = 2 * h + kci
                            inst = t.matmul(PT.t[32 * h:32 * h + 32, 0:256], colsel.t[:, 0:32],
                                            sq.t[:, ts(pc, 256)], start=(kci == 0), stop=(kci == 1),
                                            tile_position=(0, 32 * h))
                    p.post(p.fin(inst, "pe"), reads=[sq, colsel], writes=[PT])
                SEC("pe", dmm)

                def drs(a, nro=nro, PT=PT):
                    p.pre(a, "act", reads=[PT], writes=[nrm])
                    for h in range(H):
                        p.fin(a.activation(nrm.t[32 * h:32 * h + 32, nro: nro + 256],
                                           PT.t[32 * h:32 * h + 32, 0:256], AF.Ln,
                                           bias=epsv.t[32 * h:32 * h + 32, 1:2]), "act")
                        t = p.fin(a.activation(nrm.t[32 * h:32 * h + 32, nro: nro + 256],
                                               nrm.t[32 * h:32 * h + 32, nro: nro + 256],
                                               AF.Exp, scale=epsv.t[32 * h:32 * h + 32, 2:3]), "act")
                    p.post(t, reads=[PT], writes=[nrm])
                SEC("act", drs)

                for h in range(H):
                    PTb = [PC, PD][h % 2]

                    def dbc(t, h=h, nro=nro, PTb=PTb, PT=PT):
                        p.pre(t, "pe", reads=[nrm, rself], writes=[PTb])
                        inst = t.matmul(PTb.t[:, 256:512], rself.t[:, ts(h, 128)],
                                        nrm.t[:, nro: nro + 256], start=True, stop=True)
                        p.post(p.fin(inst, "pe"), reads=[nrm, rself], writes=[PTb])
                    SEC("pe", dbc)

                    def dap(v, h=h, dstf=dstf, PTb=PTb):
                        p.pre(v, "dve", reads=[PTb], writes=[dstf])
                        for kci in range(2):
                            pc = 2 * h + kci
                            t = p.fin(v.tensor_mul(dstf.t[:, ts(pc, 256)], dstf.t[:, ts(pc, 256)],
                                                   PTb.t[:, 256:512]), "dve")
                        p.post(t, reads=[PTb], writes=[dstf])
                    SEC("dve", dap)

            for h in range(H):
                PTb = [PC, PD][h % 2]

                def dbb(t, h=h, PTb=PTb):
                    p.pre(t, "pe", reads=[betap, rselm], writes=[PTb])
                    inst = t.matmul(PTb.t[:, 256:512], rselm.t[:, ts(h, 128)],
                                    betap.t[:, 0:256], start=True, stop=True)
                    p.post(p.fin(inst, "pe"), reads=[betap, rselm], writes=[PTb])
                SEC("pe", dbb)

                def dkb(v, h=h, PTb=PTb):
                    p.pre(v, "dve", reads=[PTb, kf, vf], writes=[kbn, sq])
                    for kci in range(2):
                        pc = 2 * h + kci
                        p.fin(v.tensor_mul(kbn.t[:, ts(pc, 256)], kf.t[:, ts(pc, 256)],
                                           PTb.t[:, 256:512]), "dve")
                        t = p.fin(v.scalar_tensor_tensor(sq.t[:, ts(pc, 256)],
                                                         vf.t[:, pc * 288 + 32: (pc + 1) * 288],
                                                         -1.0, PTb.t[:, 256:512],
                                                         op0=OP.mult, op1=OP.mult), "dve")
                    p.post(t, reads=[PTb, kf, vf], writes=[kbn, sq])
                SEC("dve", dkb)

            if upto == "norm":
                return
            def est(g):
                for pc in range(8):
                    DMA(g, "qkv", Dk, scr_k[ts(pc, 128), sctk(sc)], kf, kf.t[:, ts(pc, 256)])
                    DMA(g, "qkv", Dv, scr_v[ts(pc, 128), sctk(sc)], sq, sq.t[:, ts(pc, 256)])
                p.endburst("qkv", Dk, Dv, kf, sq)
                if full:
                    for tn, bsrc in (("q", qf), ("k", kf), ("v", None), ("kb", kbn)):
                        if tn in TAPS:
                            for pc in range(8):
                                srcap = (vf.t[:, pc * 288 + 32: (pc + 1) * 288] if tn == "v"
                                         else bsrc.t[:, ts(pc, 256)])
                                DMA(g, "qo", None, TAPS[tn][ts(pc, 128), sctk(sc)],
                                    bsrc if tn != "v" else vf, srcap)
                    if "beta" in TAPS:
                        DMA(g, "qo", None, TAPS["beta"][:, sctk(sc)], betap, betap.t[:, 0:256])
                    if "idsc" in TAPS:
                        DMA(g, "qo", None, TAPS["idsc"][:, sctk(sc)], idsc, idsc.t[:, 0:256])
            SEC("pool", est)

            if upto == "est":
                return
            def fhp(s):
                for hh in range(2):
                    for h in range(H):
                        TDMA(s, "htm", ktm, ktm.t[:, (hh * 4 + h) * 256: (hh * 4 + h + 1) * 256],
                             Dk, scr_k[ts(h, 256), sc * SCT + hh * 128: sc * SCT + (hh + 1) * 128])
                        TDMA(s, "htm", vtm, vtm.t[:, (hh * 4 + h) * 256: (hh * 4 + h + 1) * 256],
                             Dv, scr_v[ts(h, 256), sc * SCT + hh * 128: sc * SCT + (hh + 1) * 128])
            SEC("sp", fhp)
            p.endburst("htm", Dk, Dv)
            ktm.w = vtm.w = ("htm", p.cnt["htm"])
            ktm.rs = []; vtm.rs = []

        def emit_scan(sc, full, upto=None):
            def gmm(t):
                p.pre(t, "pe", reads=[kf, kbn], writes=[PC])
                for n in range(CPS):
                    for h in range(H):
                        for kci in range(2):
                            pc = 2 * h + kci
                            inst = t.matmul(PC.t[32 * h:32 * h + 32, ts(n, 32)],
                                            kf.t[:, pc * 256 + n * 32: pc * 256 + (n + 1) * 32],
                                            kbn.t[:, pc * 256 + n * 32: pc * 256 + (n + 1) * 32],
                                            start=(kci == 0), stop=(kci == 1),
                                            tile_position=(0, 32 * h))
                p.post(p.fin(inst, "pe"), reads=[kf, kbn], writes=[PC])
            SEC("pe", gmm)

            def gev(v):
                p.pre(v, "dve", reads=[PC, nmask], writes=[nst["1t"], nst["1"]])
                t = p.fin(v.tensor_mul(nst["1t"].t[:, :], PC.t[:, 0:256], nmask.t[:, :]), "dve")
                p.post(t, reads=[PC, nmask], writes=[nst["1t"]])
                t = p.fin(v.transpose(nst["1"].t[:, :], nst["1t"].t[:, :]), "dve")
                p.post(t, reads=[nst["1t"]], writes=[nst["1"]])
            SEC("dve", gev)

            if upto == "g":
                return
            for dst, lhs, rhs in (("2", "1t", "1"), ("4", "2t", "2"), ("8", "4t", "4"), ("16", "8t", "8")):
                def hmm(t, dst=dst, lhs=lhs, rhs=rhs):
                    p.pre(t, "pe", reads=[nst[lhs], nst[rhs]], writes=[PD])
                    for n in range(CPS):
                        for h in range(H):
                            inst = t.matmul(PD.t[32 * h:32 * h + 32, ts(n, 32)],
                                            nst[lhs].t[32 * h:32 * h + 32, ts(n, 32)],
                                            nst[rhs].t[32 * h:32 * h + 32, ts(n, 32)],
                                            start=True, stop=True, tile_position=(32 * h, 32 * h))
                    p.post(p.fin(inst, "pe"), reads=[nst[lhs], nst[rhs]], writes=[PD])
                SEC("pe", hmm)

                def hev(v, dst=dst):
                    p.pre(v, "dve", reads=[PD], writes=[nst[dst], nst[dst + "t"]])
                    t = p.fin(v.tensor_copy(nst[dst].t[:, :], PD.t[:, 0:256]), "dve")
                    p.post(t, reads=[PD], writes=[nst[dst]])
                    t = p.fin(v.transpose(nst[dst + "t"].t[:, :], nst[dst].t[:, :]), "dve")
                    p.post(t, reads=[nst[dst]], writes=[nst[dst + "t"]])
                SEC("dve", hev)

            def hch0(v):
                p.pre(v, "dve", reads=[nst["16t"], i32t], writes=[rt[0]])
                t = p.fin(v.tensor_add(rt[0].t[:, :], nst["16t"].t[:, :], i32t.t[:, :]), "dve")
                p.post(t, reads=[nst["16t"], i32t], writes=[rt[0]])
            SEC("dve", hch0)
            rord = [rt[0], rt[1], rt[2], rt[0], rt[1]]
            for li, kk in enumerate(["8", "4", "2", "1"]):
                src, dst = rord[li], rord[li + 1]

                def hcm(t, kk=kk, src=src):
                    p.pre(t, "pe", reads=[nst[kk], src], writes=[PD])
                    for n in range(CPS):
                        for h in range(H):
                            inst = t.matmul(PD.t[32 * h:32 * h + 32, ts(n, 32)],
                                            nst[kk].t[32 * h:32 * h + 32, ts(n, 32)],
                                            src.t[32 * h:32 * h + 32, ts(n, 32)],
                                            start=True, stop=True, tile_position=(32 * h, 32 * h))
                    p.post(p.fin(inst, "pe"), reads=[nst[kk], src], writes=[PD])
                SEC("pe", hcm)

                def hce(v, src=src, dst=dst):
                    p.pre(v, "dve", reads=[PD, src], writes=[dst])
                    t = p.fin(v.tensor_add(dst.t[:, :], PD.t[:, 0:256], src.t[:, :]), "dve")
                    p.post(t, reads=[PD, src], writes=[dst])
                SEC("dve", hce)
            tful = rord[4]
            tful_holder[0] = tful

            if upto == "t":
                return
            glist = [("g1", kbn, g1t, False, PG1)] + ([("g2", qf, g2t, True, PG2)] if full else [])
            for gn, rhsf, dstt, masked, PT in glist:
                for m in range(CPS):
                    reg = slice(0, 256) if m % 2 == 0 else slice(512, 768)

                    def imm(t, gn=gn, rhsf=rhsf, m=m, PT=PT, reg=reg):
                        p.pre(t, "pe", reads=[kf, rhsf], writes=[PT])
                        for h in range(H):
                            for kci in range(2):
                                pc = 2 * h + kci
                                inst = t.matmul(PT.t[32 * h:32 * h + 32, reg],
                                                kf.t[:, pc * 256 + m * 32: pc * 256 + (m + 1) * 32],
                                                rhsf.t[:, ts(pc, 256)],
                                                start=(kci == 0), stop=(kci == 1),
                                                tile_position=(0, 32 * h))
                        p.post(p.fin(inst, "pe"), reads=[kf, rhsf], writes=[PT])
                    SEC("pe", imm)

                    def iev(v, gn=gn, dstt=dstt, masked=masked, m=m, PT=PT, reg=reg):
                        p.pre(v, "dve", reads=[PT, g2m], writes=[dstt])
                        if masked:
                            t = p.fin(v.tensor_mul(dstt.t[:, ts(m, 256)], PT.t[:, reg],
                                                   g2m.t[:, ts(m, 256)]), "dve")
                        else:
                            t = p.fin(v.tensor_copy(dstt.t[:, ts(m, 256)], PT.t[:, reg]), "dve")
                        p.post(t, reads=[PT], writes=[dstt])
                    SEC("dve", iev)

            if upto == "i":
                return
            jstage = upto[1:] if (upto and upto.startswith("j") and len(upto) > 1) else None
            for n in range(CPS):
                hh, r = n // 4, n % 4

                def jz(t, n=n, hh=hh, r=r):
                    p.pre(t, "pe", reads=[kbn, sbfb, g1t, u2il, i32sp, vtm], writes=[PA])
                    for h in range(H):
                        mms = []
                        for kci in range(2):
                            pc = 2 * h + kci
                            mms.append((kbn.t[:, pc * 256 + n * 32: pc * 256 + (n + 1) * 32],
                                        sbfb.t[:, ts(pc, 256)], (0, 32 * h)))
                        for m in range(n):
                            mms.append((g1t.t[32 * h:32 * h + 32, m * 256 + n * 32: m * 256 + (n + 1) * 32],
                                        u2il.t[32 * h:32 * h + 32, ts(m, 256)], (32 * h, 32 * h)))
                        mms.append((i32sp.t[:, ts(n, 32)],
                                    vtm.t[:, (hh * 4 + h) * 256: (hh * 4 + h + 1) * 256],
                                    (0, 32 * h)))
                        for j, (lh, rh, tp) in enumerate(mms):
                            inst = t.matmul(PA.t[32 * h:32 * h + 32, 0:256], lh, rh,
                                            start=(j == 0), stop=(j == len(mms) - 1),
                                            tile_position=tp)
                    p.post(p.fin(inst, "pe"), reads=[kbn, sbfb, g1t, u2il, i32sp, vtm], writes=[PA])
                SEC("pe", jz)
                if jstage == "1":
                    if n == CPS - 1:
                        def jfl(v):
                            p.pre(v, "dve", reads=[PA], writes=[zsb])
                            t = p.fin(v.tensor_copy(zsb.t[:, :], PA.t[:, 0:256]), "dve")
                            p.post(t, reads=[PA], writes=[zsb])
                        SEC("dve", jfl)
                    continue

                def jze(v, n=n):
                    p.pre(v, "dve", reads=[PA], writes=[zsb])
                    t = p.fin(v.tensor_copy(zsb.t[:, :], PA.t[:, 0:256]), "dve")
                    p.post(t, reads=[PA], writes=[zsb])
                SEC("dve", jze)
                if full and sc == 0 and "zt" in TAPS:
                    def ztap(g, n=n):
                        DMA(g, "qo", None, TAPS["zt"][:, ts(n, 256)], zsb, zsb.t[:, :])
                    SEC("pool", ztap)

                def ju2(t, n=n, hh=hh, r=r):
                    p.pre(t, "pe", reads=[tful, zsb], writes=[PBp])
                    for h in range(H):
                        inst = t.matmul(PBp.t[32 * h:32 * h + 32, 0:256],
                                        tful.t[32 * h:32 * h + 32, ts(n, 32)],
                                        zsb.t[32 * h:32 * h + 32, :],
                                        start=True, stop=True, tile_position=(32 * h, 32 * h))
                    tk1 = p.fin(inst, "pe")
                    p.post(tk1, reads=[tful, zsb], writes=[PBp])
                SEC("pe", ju2)

                def ju2e(v, n=n, hh=hh):
                    p.pre(v, "dve", reads=[PBp], writes=[u2il])
                    t = p.fin(v.tensor_copy(u2il.t[:, ts(n, 256)], PBp.t[:, 0:256]), "dve")
                    p.post(t, reads=[PBp], writes=[u2il])
                SEC("dve", ju2e)

                def ju2w(g, n=n):
                    DMA(g, "qud", Du2, scr_u2[:, 32 * n:32 * n + 32, :], u2il,
                        u2il.t[:, ts(n, 256)])
                SEC("pool", ju2w)

                if n == CPS - 1:
                    def ju2r(s):
                        p.endburst("qud", Du2)
                        for hh2 in range(2):
                            for h in range(H):
                                DMA(s, "qu2", u2pl,
                                    u2pl.t[:, hh2 * 1024 + h * 256: hh2 * 1024 + (h + 1) * 256],
                                    Du2, scr_u2[h, hh2 * 128:(hh2 + 1) * 128, :], ek="sp")
                        p.endburst("qu2", Du2)
                        u2pl.w = ("qu2", p.cnt["qu2"])
                        u2pl.rs = []
                    SEC("sp", ju2r)

            if upto is not None and upto.startswith("j"):
                return
            if full and sc == 0:
                def scantap(g):
                    for tn, bsrc in (("g1t", g1t), ("g2t", g2t), ("n1t", nst["1t"]),
                                     ("tful", tful_holder[0]), ("u2", u2il), ("u2p", u2pl)):
                        if tn in TAPS:
                            DMA(g, "qo", None, TAPS[tn][:, :], bsrc, bsrc.t[:, :])
                SEC("pool", scantap)

            if full:
                for tsl in range(2):
                    for h in range(H):
                        PT = [PC, PD][(tsl * H + h) % 2]

                        def lmm(t, tsl=tsl, h=h, PT=PT):
                            p.pre(t, "pe", reads=[qf, sbfb, g2t, u2il], writes=[PT])
                            mms = []
                            for kci in range(2):
                                pc = 2 * h + kci
                                mms.append((qf.t[:, pc * 256 + tsl * 128: pc * 256 + (tsl + 1) * 128],
                                            sbfb.t[:, ts(pc, 256)], None))
                            for m in range(CPS):
                                mms.append((g2t.t[32 * h:32 * h + 32, m * 256 + tsl * 128: m * 256 + (tsl + 1) * 128],
                                            u2il.t[32 * h:32 * h + 32, ts(m, 256)], (32 * h, 0)))
                            for j, (lh, rh, tp) in enumerate(mms):
                                inst = t.matmul(PT.t[:, 0:256], lh, rh, start=(j == 0),
                                                stop=(j == len(mms) - 1), tile_position=tp)
                            p.post(p.fin(inst, "pe"), reads=[qf, sbfb, g2t, u2il], writes=[PT])
                        SEC("pe", lmm)

                        def lev(v, tsl=tsl, h=h, PT=PT):
                            p.pre(v, "dve", reads=[PT], writes=[hmid])
                            t = p.fin(v.tensor_copy(hmid.t[:, tsl * 1024 + h * 256: tsl * 1024 + (h + 1) * 256],
                                                    PT.t[:, 0:256]), "dve")
                            p.post(t, reads=[PT], writes=[hmid])
                        SEC("dve", lev)

                def ldma(g):
                    for tsl in range(2):
                        DMA(g, "qd", Dd, scr_d[sc * SCT + tsl * 128: sc * SCT + (tsl + 1) * 128, :],
                            hmid, hmid.t[:, ts(tsl, 1024)])
                        if "delta" in TAPS:
                            DMA(g, "qo", None, TAPS["delta"][sc * SCT + tsl * 128: sc * SCT + (tsl + 1) * 128, :],
                                hmid, hmid.t[:, ts(tsl, 1024)])
                SEC("pool", ldma)
                p.endburst("qd", Dd, hmid)

                def lhop(s):
                    for pc in range(8):
                        TDMA(s, "hd", dfm, dfm.t[:, ts(pc, 256)],
                             Dd, scr_d[sctk(sc), ts(pc, 128)])
                SEC("sp", lhop)
                p.endburst("hd", Dd)
                dfm.w = ("hd", p.cnt["hd"]); dfm.rs = []

            for h in range(H):
                for dks in range(2):
                    PT = [PC, PD][(h * 2 + dks) % 2]

                    def kmm(t, h=h, dks=dks, PT=PT):
                        p.pre(t, "pe", reads=[ktm, u2pl], writes=[PT])
                        for hh in range(2):
                            inst = t.matmul(PT.t[:, 0:256],
                                            ktm.t[:, (hh * 4 + h) * 256 + dks * 128: (hh * 4 + h) * 256 + (dks + 1) * 128],
                                            u2pl.t[:, hh * 1024 + h * 256: hh * 1024 + (h + 1) * 256],
                                            start=(hh == 0), stop=(hh == 1))
                        p.post(p.fin(inst, "pe"), reads=[ktm, u2pl], writes=[PT])
                    SEC("pe", kmm)

                    def kev(v, h=h, dks=dks, PT=PT):
                        blkc = (2 * h + dks) * 256
                        p.pre(v, "dve", reads=[PT], writes=[sfp, sbfb])
                        p.fin(v.tensor_add(sfp.t[:, blkc:blkc + 256], sfp.t[:, blkc:blkc + 256],
                                           PT.t[:, 0:256]), "dve")
                        t = p.fin(v.tensor_copy(sbfb.t[:, blkc:blkc + 256], sfp.t[:, blkc:blkc + 256]), "dve")
                        p.post(t, reads=[PT], writes=[sfp, sbfb])
                    SEC("dve", kev)

        def emit_tail(sc, upto=None):
            sco = sctk(sc)
            for pc in range(8):
                def tfir(v, pc=pc):
                    p.pre(v, "dve", reads=[vf, firw], writes=[fac, fsf, sq])
                    p.fin(v.tensor_scalar_mul(fac.t[:, 0:256], vf.t[:, pc * 288 + 30: pc * 288 + 286],
                                              firw.t[:, pc * 3: pc * 3 + 1]), "dve")
                    for j in range(1, 3):
                        p.fin(v.scalar_tensor_tensor(fac.t[:, 0:256],
                                                     vf.t[:, pc * 288 + 30 + j: pc * 288 + 286 + j],
                                                     firw.t[:, pc * 3 + j: pc * 3 + j + 1],
                                                     fac.t[:, 0:256], op0=OP.mult, op1=OP.add), "dve")
                    t = p.fin(v.tensor_copy(fsf.t[:, ts(pc, 256)], fac.t[:, 0:256]), "dve")
                    p.post(t, reads=[vf, firw], writes=[fsf])
                    t = p.fin(v.tensor_mul(sq.t[:, ts(pc, 256)], fac.t[:, 0:256], fac.t[:, 0:256]), "dve")
                    p.post(t, reads=[], writes=[sq, fac])
                SEC("dve", tfir)
            def tst1(t):
                p.pre(t, "pe", reads=[fsf, sq, statsel], writes=[PC, PD])
                for pc in range(8):
                    i1 = t.matmul(PC.t[0:12, 0:256], statsel.t[:, 0 + pc * 12: 0 + (pc + 1) * 12],
                                  fsf.t[:, ts(pc, 256)], start=(pc == 0), stop=False)
                tk = p.fin(i1, "pe"); p.post(tk, reads=[fsf, statsel], writes=[PC])
                for pc in range(8):
                    i2 = t.matmul(PD.t[0:12, 0:256], statsel.t[:, 0 + pc * 12: 0 + (pc + 1) * 12],
                                  sq.t[:, ts(pc, 256)], start=(pc == 0), stop=False)
                tk = p.fin(i2, "pe"); p.post(tk, reads=[sq, statsel], writes=[PD])
            SEC("pe", tst1)

            for pc in range(8):
                def tfl(v, pc=pc):
                    p.pre(v, "dve", reads=[vf, firw], writes=[fac])
                    p.fin(v.tensor_scalar_mul(fac.t[:, 0:256], vf.t[:, pc * 288 + 2: pc * 288 + 258],
                                              firw.t[:, 24 + pc * 31: 24 + pc * 31 + 1]), "dve")
                    for j in range(1, 31):
                        t = p.fin(v.scalar_tensor_tensor(fac.t[:, 0:256],
                                                         vf.t[:, pc * 288 + 2 + j: pc * 288 + 258 + j],
                                                         firw.t[:, 24 + pc * 31 + j: 24 + pc * 31 + j + 1],
                                                         fac.t[:, 0:256], op0=OP.mult, op1=OP.add), "dve")
                    p.post(t, reads=[vf, firw], writes=[fac])
                SEC("dve", tfl)

                def tflm(v, pc=pc):
                    p.pre(v, "dve", reads=[fac], writes=[flf, sq])
                    t = p.fin(v.tensor_copy(flf.t[:, ts(pc, 256)], fac.t[:, 0:256]), "dve")
                    p.post(t, reads=[fac], writes=[flf])
                    t = p.fin(v.tensor_mul(sq.t[:, ts(pc, 256)], flf.t[:, ts(pc, 256)],
                                           flf.t[:, ts(pc, 256)]), "dve")
                    p.post(t, reads=[flf], writes=[sq])
                SEC("dve", tflm)

            def tst2(t):
                p.pre(t, "pe", reads=[flf, sq, statsel], writes=[PC, PD])
                for pc in range(8):
                    i1 = t.matmul(PC.t[0:12, 0:256], statsel.t[:, 96 + pc * 12: 96 + (pc + 1) * 12],
                                  flf.t[:, ts(pc, 256)], start=False, stop=False)
                tk = p.fin(i1, "pe"); p.post(tk, reads=[flf, statsel], writes=[PC])
                for pc in range(8):
                    i2 = t.matmul(PD.t[0:12, 0:256], statsel.t[:, 96 + pc * 12: 96 + (pc + 1) * 12],
                                  sq.t[:, ts(pc, 256)], start=False, stop=False)
                tk = p.fin(i2, "pe"); p.post(tk, reads=[sq, statsel], writes=[PD])
            SEC("pe", tst2)

            for pc in range(8):
                def tdsq(v, pc=pc):
                    p.pre(v, "dve", reads=[dfm], writes=[sq])
                    t = p.fin(v.tensor_mul(sq.t[:, ts(pc, 256)], dfm.t[:, ts(pc, 256)],
                                           dfm.t[:, ts(pc, 256)]), "dve")
                    p.post(t, reads=[dfm], writes=[sq])
                SEC("dve", tdsq)

            def tst3(t):
                p.pre(t, "pe", reads=[dfm, sq, statsel], writes=[PC, PD])
                for pc in range(8):
                    i1 = t.matmul(PC.t[0:12, 0:256], statsel.t[:, 192 + pc * 12: 192 + (pc + 1) * 12],
                                  dfm.t[:, ts(pc, 256)], start=False, stop=(pc == 7))
                tk = p.fin(i1, "pe"); p.post(tk, reads=[dfm, statsel], writes=[PC])
                for pc in range(8):
                    i2 = t.matmul(PD.t[0:12, 0:256], statsel.t[:, 192 + pc * 12: 192 + (pc + 1) * 12],
                                  sq.t[:, ts(pc, 256)], start=False, stop=(pc == 7))
                tk = p.fin(i2, "pe"); p.post(tk, reads=[sq, statsel], writes=[PD])
            SEC("pe", tst3)

            def tms(v):
                p.pre(v, "dve", reads=[PC, PD], writes=[nrm, smeanb])
                p.fin(v.tensor_copy(nrm.t[0:12, 0:256], PC.t[0:12, 0:256]), "dve")
                p.fin(v.tensor_mul(nrm.t[0:12, 256:512], nrm.t[0:12, 0:256], nrm.t[0:12, 0:256]), "dve")
                p.fin(v.scalar_tensor_tensor(nrm.t[0:12, 256:512], PD.t[0:12, 0:256], 1.0,
                                             nrm.t[0:12, 256:512], op0=OP.mult, op1=OP.subtract), "dve")
                p.fin(v.tensor_scalar_max(nrm.t[0:12, 256:512], nrm.t[0:12, 256:512], 0.0), "dve")
                t = p.fin(v.tensor_copy(smeanb.t[:, :], nrm.t[0:12, 0:256]), "dve")
                p.post(t, reads=[PC, PD], writes=[nrm, smeanb])
            SEC("dve", tms)

            def tsd(a):
                p.pre(a, "act", reads=[nrm], writes=[sstdb])
                t = p.fin(a.activation(sstdb.t[:, :], nrm.t[0:12, 256:512], AF.Sqrt), "act")
                p.post(t, reads=[nrm], writes=[sstdb])
            SEC("act", tsd)

            if "smean" in TAPS or "sstd" in TAPS:
                def tstap(g):
                    if "smean" in TAPS:
                        DMA(g, "qo", None, TAPS["smean"][:, sco], smeanb, smeanb.t[:, :])
                    if "sstd" in TAPS:
                        DMA(g, "qo", None, TAPS["sstd"][:, sco], sstdb, sstdb.t[:, :])
                SEC("pool", tstap)
            if "fs" in TAPS or "fl" in TAPS:
                def tftap(g):
                    for pc in range(8):
                        if "fs" in TAPS:
                            DMA(g, "qo", None, TAPS["fs"][ts(pc, 128), sco], fsf, fsf.t[:, ts(pc, 256)])
                        if "fl" in TAPS:
                            DMA(g, "qo", None, TAPS["fl"][ts(pc, 128), sco], flf, flf.t[:, ts(pc, 256)])
                SEC("pool", tftap)

            if upto == "stats":
                return
            for half in range(2):
                for fci in range(8):
                    fc = half * 8 + fci
                    PT = [PA, PBp][fci % 2]

                    def rmm(t, fc=fc, PT=PT):
                        p.pre(t, "pe", reads=[wr1, xnf, wr1sm, wr1ss, smeanb, sstdb], writes=[PT])
                        for kc in range(8):
                            t.matmul(PT.t[:, 0:256], wr1.t[:, kc * 2048 + fc * 128: kc * 2048 + (fc + 1) * 128],
                                     xnf.t[:, kc * 384 + 128: (kc + 1) * 384], start=(kc == 0), stop=False)
                        t.matmul(PT.t[:, 0:256], wr1sm.t[:, ts(fc, 128)], smeanb.t[:, :],
                                 start=False, stop=False)
                        inst = t.matmul(PT.t[:, 0:256], wr1ss.t[:, ts(fc, 128)], sstdb.t[:, :],
                                        start=False, stop=True)
                        p.post(p.fin(inst, "pe"), reads=[wr1, xnf, wr1sm, wr1ss, smeanb, sstdb], writes=[PT])
                    SEC("pe", rmm)

                    def rge(a, fc=fc, fci=fci, PT=PT):
                        p.pre(a, "act", reads=[PT, br1], writes=[hmid])
                        t = p.fin(a.activation(hmid.t[:, ts(fci, 256)], PT.t[:, 0:256], AF.Gelu,
                                               bias=br1.t[:, fc:fc + 1]), "act")
                        p.post(t, reads=[PT, br1], writes=[hmid])
                    SEC("act", rge)

                def rlg(t, half=half):
                    p.pre(t, "pe", reads=[wr2, hmid], writes=[PC])
                    for kci in range(8):
                        kc = half * 8 + kci
                        inst = t.matmul(PC.t[0:12, 0:256], wr2.t[:, ts(kc, 12)], hmid.t[:, ts(kci, 256)],
                                        start=(kc == 0), stop=(kc == 15))
                    p.post(p.fin(inst, "pe"), reads=[wr2, hmid], writes=[PC])
                SEC("pe", rlg)

            if upto == "rtr":
                return
            def rls(v):
                p.pre(v, "dve", reads=[PC, br2, it12], writes=[lgp])
                t = p.fin(v.tensor_scalar(lgp.t[0:12, :], PC.t[0:12, 0:256], br2.t[:, 0:1],
                                          it12.t[:, 0:1], op0=OP.add, op1=OP.mult), "dve")
                p.post(t, reads=[PC, br2, it12], writes=[lgp])
            SEC("dve", rls)

            if upto == "p1":
                return
            def rlw(g):
                DMA(g, "qlg", Dlg, scr_lg[:, :], lgp, lgp.t[:, :])
            SEC("pool", rlw)

            if upto == "p2a":
                return
            for hh in range(2):
                def rhp(s, hh=hh):
                    TDMA(s, "hlg", ph_, ph_.t[:, ts(hh, 16)], Dlg, scr_lg[:, ts(hh, 128)])
                SEC("sp", rhp)

                if upto == "p2":
                    continue
                def rsm(v, hh=hh):
                    p.pre(v, "dve", reads=[ph_], writes=[ptf, mxg, dif, exg])
                    p.fin(v.tensor_copy(ptf.t[:, 0:12], ph_.t[:, hh * 16: hh * 16 + 12]), "dve")
                    p.fin(v.tensor_max(mxg.t[:, :], ptf.t[:, 0:12:3], ptf.t[:, 1:12:3]), "dve")
                    p.fin(v.tensor_max(mxg.t[:, :], mxg.t[:, :], ptf.t[:, 2:12:3]), "dve")
                    mb = bass.AP(mxg.t, 0, [[4, 128], [1, 4], [0, 3]])
                    t = p.fin(v.tensor_sub(dif.t[:, 0:12], ptf.t[:, 0:12], mb), "dve")
                    p.post(t, reads=[ph_], writes=[ptf, mxg, dif])
                SEC("dve", rsm)

                if upto == "p3":
                    continue
                def rex(a, hh=hh):
                    p.pre(a, "act", reads=[dif], writes=[exg])
                    t = p.fin(a.activation(exg.t[:, 0:12], dif.t[:, 0:12], AF.Exp), "act")
                    p.post(t, reads=[dif], writes=[exg])
                SEC("act", rex)

                if upto == "p4":
                    continue
                def rp2(v, hh=hh):
                    p.pre(v, "dve", reads=[exg], writes=[sumr, ptmb])
                    p.fin(v.tensor_add(sumr.t[:, :], exg.t[:, 0:12:3], exg.t[:, 1:12:3]), "dve")
                    p.fin(v.tensor_add(sumr.t[:, :], sumr.t[:, :], exg.t[:, 2:12:3]), "dve")
                    p.fin(v.reciprocal(sumr.t[:, :], sumr.t[:, :]), "dve")
                    sb_ = bass.AP(sumr.t, 0, [[4, 128], [1, 4], [0, 3]])
                    p.fin(v.tensor_mul(dif.t[:, 0:12], exg.t[:, 0:12], sb_), "dve")
                    t = p.fin(v.tensor_scalar(ptmb.t[:, 0:12], dif.t[:, 0:12],
                                              1.0 - 3 * 0.025, 0.025, op0=OP.mult, op1=OP.add), "dve")
                    p.post(t, reads=[exg], writes=[sumr, ptmb, dif])
                SEC("dve", rp2)

                if upto == "p5":
                    continue
                def rpw(g, hh=hh):
                    DMA(g, "qlg", Dp, scr_p[ts(hh, 128), :], ptmb, ptmb.t[:, :])
                SEC("pool", rpw)

                if upto == "p6":
                    continue
                def rph(s, hh=hh):
                    TDMA(s, "hp", pfm, pfm.t[:, ts(hh, 128)], Dp, scr_p[ts(hh, 128), :])
                SEC("sp", rph)
            if "p" in TAPS:
                def ptap(g):
                    DMA(g, "qo", None, TAPS["p"][:, sco], pfm, pfm.t[0:16, :])
                SEC("pool", ptap)

            if upto == "p":
                return
            for h in range(H):
                def mbc(t, h=h):
                    p.pre(t, "pe", reads=[gatesel, pfm, rsel, idsc], writes=[PG2])
                    for gg in range(3):
                        inst = t.matmul(PG2.t[:, ts(gg, 256)], gatesel.t[:, ts(3 * h + gg, 128)],
                                        pfm.t[:, :], start=True, stop=True)
                    inst = t.matmul(PG2.t[:, 768:1024], rsel.t[:, ts(h, 128)],
                                    idsc.t[:, 0:256], start=True, stop=True)
                    p.post(p.fin(inst, "pe"), reads=[gatesel, pfm, rsel, idsc], writes=[PG2])
                SEC("pe", mbc)

                if upto == "m1":
                    continue
                def mge(v, h=h):
                    p.pre(v, "dve", reads=[PG2], writes=[gbh])
                    t = p.fin(v.tensor_copy(gbh.t[:, :], PG2.t[:, 0:1024]), "dve")
                    p.post(t, reads=[PG2], writes=[gbh])
                SEC("dve", mge)

                for kci in range(2):
                    pc = 2 * h + kci

                    def mmx(v, h=h, kci=kci, pc=pc):
                        p.pre(v, "dve", reads=[fsf, flf, dfm, vf, gbh], writes=[of32, tmx[0], sq])
                        oc = slice(kci * 256, (kci + 1) * 256)
                        p.fin(v.tensor_mul(of32.t[:, oc], fsf.t[:, ts(pc, 256)], gbh.t[:, 0:256]), "dve")
                        p.fin(v.tensor_mul(tmx[0].t[:, :], flf.t[:, ts(pc, 256)], gbh.t[:, 256:512]), "dve")
                        p.fin(v.tensor_add(of32.t[:, oc], of32.t[:, oc], tmx[0].t[:, :]), "dve")
                        p.fin(v.tensor_mul(tmx[0].t[:, :], dfm.t[:, ts(pc, 256)], gbh.t[:, 512:768]), "dve")
                        p.fin(v.tensor_add(of32.t[:, oc], of32.t[:, oc], tmx[0].t[:, :]), "dve")
                        p.fin(v.tensor_mul(tmx[0].t[:, :], vf.t[:, pc * 288 + 32: (pc + 1) * 288],
                                           gbh.t[:, 768:1024]), "dve")
                        p.fin(v.tensor_add(of32.t[:, oc], of32.t[:, oc], tmx[0].t[:, :]), "dve")
                        t = p.fin(v.tensor_mul(sq.t[:, ts(pc, 256)], of32.t[:, oc], of32.t[:, oc]), "dve")
                        p.post(t, reads=[fsf, flf, dfm, vf, gbh], writes=[of32, tmx[0], sq])
                    SEC("dve", mmx)

                if upto == "m2":
                    continue
                def mrm(t, h=h):
                    p.pre(t, "pe", reads=[sq, colsel], writes=[PC])
                    for kci in range(2):
                        pc = 2 * h + kci
                        inst = t.matmul(PC.t[32 * h:32 * h + 32, 0:256], colsel.t[:, 32:64],
                                        sq.t[:, ts(pc, 256)], start=(kci == 0), stop=(kci == 1),
                                        tile_position=(0, 32 * h))
                    p.post(p.fin(inst, "pe"), reads=[sq, colsel], writes=[PC])
                SEC("pe", mrm)

                if upto == "m3":
                    continue
                def mrs(a, h=h):
                    p.pre(a, "act", reads=[PC], writes=[nrm])
                    p.fin(a.activation(nrm.t[32 * h:32 * h + 32, 0:256], PC.t[32 * h:32 * h + 32, 0:256],
                                       AF.Ln, bias=epsv.t[32 * h:32 * h + 32, 0:1]), "act")
                    t = p.fin(a.activation(nrm.t[32 * h:32 * h + 32, 0:256], nrm.t[32 * h:32 * h + 32, 0:256],
                                           AF.Exp, scale=epsv.t[32 * h:32 * h + 32, 2:3]), "act")
                    p.post(t, reads=[PC], writes=[nrm])
                SEC("act", mrs)

                if upto == "m4":
                    continue
                def mob(t, h=h):
                    p.pre(t, "pe", reads=[nrm, rself], writes=[PD])
                    inst = t.matmul(PD.t[:, 0:256], rself.t[:, ts(h, 128)],
                                    nrm.t[:, 0:256], start=True, stop=True)
                    p.post(p.fin(inst, "pe"), reads=[nrm, rself], writes=[PD])
                SEC("pe", mob)

                if upto == "m5":
                    continue
                def moa(v, h=h):
                    p.pre(v, "dve", reads=[PD, of32, onw], writes=[off])
                    for kci in range(2):
                        pc = 2 * h + kci
                        t = p.fin(v.scalar_tensor_tensor(off.t[:, ts(pc, 256)],
                                                         of32.t[:, kci * 256:(kci + 1) * 256],
                                                         onw.t[:, kci:kci + 1], PD.t[:, 0:256],
                                                         op0=OP.mult, op1=OP.mult), "dve")
                    p.post(t, reads=[PD, of32, onw], writes=[off])
                SEC("dve", moa)
            if "o" in TAPS:
                def otap(g):
                    for pc in range(8):
                        DMA(g, "qo", None, TAPS["o"][ts(pc, 128), sco], off, off.t[:, ts(pc, 256)])
                SEC("pool", otap)

            if upto == "mix":
                return
            for tsl in range(2):
                b = 0

                def wx(s, tsl=tsl, b=b):
                    DMA(s, "qx", xres[0], xres[0].t[:, :],
                        None, x_h[HALO + sc * SCT + tsl * 128: HALO + sc * SCT + (tsl + 1) * 128, :], ek="sp")
                SEC("sp", wx)
                if tsl == 0:
                    PQ = [PA, PBp, PC, PD]
                    for kc in range(8):
                        wb = wob[kc % 2]

                        def wld(g, kc=kc, wb=wb):
                            DMA(g, ("qb0" if kc % 2 == 0 else "qb1"), wb, wb.t[:, :],
                                None, w_h["wo"][ts(kc, 128), :])
                        SEC("pool", wld)

                        def wmm(t, kc=kc, wb=wb):
                            p.pre(t, "pe", reads=[off, wb], writes=PQ if kc == 0 else [])
                            for tsl2 in range(2):
                                for nch in range(2):
                                    inst = t.matmul(PQ[tsl2 * 2 + nch].t[:, 0:512],
                                                    off.t[:, kc * 256 + tsl2 * 128: kc * 256 + (tsl2 + 1) * 128],
                                                    wb.t[:, ts(nch, 512)],
                                                    start=(kc == 0), stop=(kc == 7))
                            tk = p.fin(inst, "pe")
                            p.post(tk, reads=[off, wb], writes=PQ if kc == 7 else [])
                        SEC("pe", wmm)
                PAx, PBx = (PA, PBp) if tsl == 0 else (PC, PD)

                def wlnX(v, tsl=tsl, PAx=PAx, PBx=PBx):
                    p.pre(v, "dve", reads=[PAx, PBx], writes=[mv])
                    p.fin(v.bn_stats(mv.t[:, 0:6], PAx.t[:, 0:512]), "dve")
                    p.fin(v.bn_stats(mv.t[:, 6:12], PBx.t[:, 0:512]), "dve")
                    t = p.fin(v.bn_aggr(mv.t[:, 12:14], mv.t[:, 0:12]), "dve")
                    p.post(t, reads=[PAx, PBx], writes=[mv])
                SEC("dve", wlnX)

                def wrs(a):
                    p.pre(a, "act", reads=[mv], writes=[mv])
                    p.fin(a.activation(mv.t[:, 14:15], mv.t[:, 13:14], AF.Ln, bias=epsv.t[:, 0:1]), "act")
                    t = p.fin(a.activation(mv.t[:, 15:16], mv.t[:, 14:15], AF.Exp, scale=epsv.t[:, 2:3]), "act")
                    p.post(t, writes=[mv])
                SEC("act", wrs)

                def wh1(v, tsl=tsl, PAx=PAx, PBx=PBx):
                    p.pre(v, "dve", reads=[PAx, PBx, mv, xres[0]], writes=[sc1, h1t[0]])
                    p.fin(v.tensor_scalar(sc1.t[:, 0:512], PAx.t[:, 0:512], mv.t[:, 12:13],
                                          mv.t[:, 15:16], op0=OP.subtract, op1=OP.mult), "dve")
                    p.fin(v.tensor_scalar(sc1.t[:, 512:1024], PBx.t[:, 0:512], mv.t[:, 12:13],
                                          mv.t[:, 15:16], op0=OP.subtract, op1=OP.mult), "dve")
                    t = p.fin(v.tensor_add(h1t[0].t[:, :], sc1.t[:, :], xres[0].t[:, :]), "dve")
                    p.post(t, reads=[PAx, PBx, mv, xres[0]], writes=[sc1, h1t[0]])
                    p.pre(v, "dve", reads=[h1t[0]], writes=[mv])
                    p.fin(v.bn_stats(mv.t[:, 0:6], h1t[0].t[:, 0:512]), "dve")
                    p.fin(v.bn_stats(mv.t[:, 6:12], h1t[0].t[:, 512:1024]), "dve")
                    t = p.fin(v.bn_aggr(mv.t[:, 12:14], mv.t[:, 0:12]), "dve")
                    p.post(t, reads=[h1t[0]], writes=[mv])
                SEC("dve", wh1)

                def wr4(a):
                    p.pre(a, "act", reads=[mv], writes=[mv])
                    p.fin(a.activation(mv.t[:, 14:15], mv.t[:, 13:14], AF.Ln, bias=epsv.t[:, 0:1]), "act")
                    t = p.fin(a.activation(mv.t[:, 15:16], mv.t[:, 14:15], AF.Exp, scale=epsv.t[:, 2:3]), "act")
                    p.post(t, writes=[mv])
                SEC("act", wr4)

                def wxi(v, tsl=tsl):
                    p.pre(v, "dve", reads=[h1t[0], mv], writes=[xint[0]])
                    t = p.fin(v.tensor_scalar(xint[0].t[:, :], h1t[0].t[:, :], mv.t[:, 12:13],
                                              mv.t[:, 15:16], op0=OP.subtract, op1=OP.mult), "dve")
                    p.post(t, reads=[h1t[0], mv], writes=[xint[0]])
                SEC("dve", wxi)

                def wst(g, tsl=tsl):
                    rows = slice(sc * SCT + tsl * 128, sc * SCT + (tsl + 1) * 128)
                    DMA(g, "qhx", Dh1, scr_h1[rows, :], h1t[0], h1t[0].t[:, :])
                    DMA(g, "qhx", Dxin, scr_xin[rows, :], xint[0], xint[0].t[:, :])

                    if "h1" in TAPS:
                        DMA(g, "qo", None, TAPS["h1"][rows, :], h1t[0], h1t[0].t[:, :])
                    if "xin" in TAPS:
                        DMA(g, "qo", None, TAPS["xin"][rows, :], xint[0], xint[0].t[:, :])
                SEC("pool", wst)
                p.endburst("qhx", Dh1, Dxin, h1t[0], xint[0])

        fine = None
        if stop_after and stop_after.startswith("f_"):
            fine = stop_after[2:]
        nsc1 = 1 if fine else NSC
        for sc in range(nsc1):
            emit_sc(sc, False, upto=fine)
            if fine not in ("proj", "beta", "norm", "est"):
                emit_scan(sc, False, upto=fine)
        if fine:
            early_out()
            return nc

        if stop_after == "pass1":
            early_out()
            return nc

        def ccpre(v):
            p.pre(v, "dve", reads=[hflagn], writes=[sfp])
            t = p.fin(v.tensor_scalar_mul(sfp.t[:, :], sfp.t[:, :], hflagn.t[:, 0:1]), "dve")
            p.post(t, reads=[hflagn], writes=[sfp])
        SEC("dve", ccpre)

        def ccw(g):
            p.pre(g, "pool", reads=[sfp], writes=[Dci])
            tok = p.dmatok(g.dma_start(out=cc_i.rearrange("(b q) d -> q b d", q=128)[:, :, :],
                                       in_=sfp.t.rearrange("q (b d) -> q b d", b=8)[:, :, :]), "qg")
            p.post(tok, reads=[sfp], writes=[Dci])
        SEC("pool", ccw)

        def ccr(g):
            p.pre(g, "pool", reads=[Dci], writes=[Dco])
            inst = g.collective_compute("AllReduce", OP.add,
                                        replica_groups=[[2 * i, 2 * i + 1] for i in range(pairs)],
                                        ins=[cc_i.ap()], outs=[cc_o.ap()])
            inst.then_inc(p.sem["cc"], 1)
            p.cnt["cc"] += 1
            p.post(("cc", p.cnt["cc"]), reads=[Dci], writes=[Dco])
        SEC("pool", ccr)

        def ccl(g):
            p.pre(g, "pool", reads=[Dco], writes=[sfp])
            tok = p.dmatok(g.dma_start(out=sfp.t.rearrange("q (b d) -> q b d", b=8)[:, :, :],
                                       in_=cc_o.rearrange("(b q) d -> q b d", q=128)[:, :, :]), "qg")
            p.post(tok, reads=[Dco], writes=[sfp])
        SEC("pool", ccl)

        def ccm(v):
            p.pre(v, "dve", reads=[sfp, hflag], writes=[sfp, sbfb])
            p.fin(v.tensor_scalar_mul(sfp.t[:, :], sfp.t[:, :], hflag.t[:, 0:1]), "dve")
            t = p.fin(v.tensor_copy(sbfb.t[:, :], sfp.t[:, :]), "dve")
            p.post(t, reads=[hflag], writes=[sfp, sbfb])
            if "send" in TAPS:
                pass
        SEC("dve", ccm)
        if "send" in TAPS:
            def sstap(g):
                DMA(g, "qo", None, TAPS["send"][:, :], sfp, sfp.t[:, :])
            SEC("pool", sstap)

        if stop_after == "cc":
            early_out()
            return nc

        tfine = stop_after[2:] if (stop_after and stop_after.startswith("t_")) else None
        for sc in range(NSC):
            emit_sc(sc, True)
            emit_scan(sc, True)
            if stop_after != "pass2scan":
                emit_tail(sc, upto=tfine)
        if stop_after in ("pass2scan", "tail") or tfine:
            early_out()
            return nc

        ats.close()

        snap = dict(p.cnt)

        def _fence(e, ek):
            for s_, v_ in snap.items():
                if v_ > 0:
                    p._w(e, ek, (s_, v_))
        for ek_ in ("pe", "dve", "act", "pool", "sp"):
            SEC(ek_, lambda e, ek_=ek_: _fence(e, ek_))

        xing = sb("xing_s", [128, 4 * D])
        xf = sb("xf_s", [128, 8 * 512])
        w1c = [sb(f"w1c{i}", [128, 8 * 512]) for i in range(2)]
        w2c = [sb(f"w2c{i}", [128, 4 * D]) for i in range(2)]
        hgrp = sb("hgrp_s", [128, 32 * 512])
        affb = sb("aff_s", [128, 8], F32)
        smax = sb("smax_s", [128, 8], F32)
        sidx = Bf(es.enter_context(nc.sbuf_tensor("sidx_s", [128, 8], U32)))
        eidf = sb("eidf_s", [128, 4], F32)
        alphag = sb("alphag_s", [128, 4], F32)
        maskg = sb("maskg_s", [128, 32], F32)
        gtok = sb("gtok_s", [128, D], F32)
        btok = sb("btok_s", [128, D], F32)
        accg = sb("accg_s", [128, 4 * D], F32)
        of32f = sb("of32f_s", [128, D], F32)
        cens = sb("cens_s", [128, 64])
        b1r = sb("b1r_s", [128, 256], F32)
        b2p = sb("b2p_s", [128, 2 * D])
        xlt = [sb(f"xlt{i}", [128, D]) for i in range(2)]

        def ffn_ld(g):
            ld = lambda o_ap, i_ap: p.dmatok(g.dma_start(out=o_ap, in_=i_ap), "qw")
            for kc in range(8):
                ld(cens.t[:, ts(kc, 8)], cen_h[ts(kc, 128), :])
            ld(b1r.t[:, :], b1_h[:, :])
            ld(b2p.t[:, 0:D], b2p_h[0:128, :]); ld(b2p.t[:, D:2 * D], b2p_h[128:256, :])
            tok = ("qw", p.cnt["qw"])
            for b in (cens, b1r, b2p):
                b.w = tok
        SEC("pool", ffn_ld)

        for g_ in range(NG):
            grows = slice(g_ * 512, (g_ + 1) * 512)

            def n_xload(s, g_=g_):
                for tl in range(4):
                    DMA(s, "qx", xing, xing.t[:, ts(tl, D)],
                        Dxin, scr_xin[g_ * 512 + tl * 128: g_ * 512 + (tl + 1) * 128, :], ek="sp")
            SEC("sp", n_xload)
            p.endburst("qx", Dxin)
            xing.w = ("qx", p.cnt["qx"]); xing.rs = []

            def n_xhop(s, g_=g_):
                for pc in range(8):
                    TDMA(s, "hxf", xf, xf.t[:, ts(pc, 512)], Dxin, scr_xin[grows, ts(pc, 128)])
            SEC("sp", n_xhop)
            p.endburst("hxf", Dxin)
            xf.w = ("hxf", p.cnt["hxf"]); xf.rs = []

            for tl in range(4):
                def n_aff(t, tl=tl):
                    p.pre(t, "pe", reads=[xf, cens], writes=[PC])
                    for kc in range(8):
                        inst = t.matmul(PC.t[:, 0:8], xf.t[:, kc * 512 + tl * 128: kc * 512 + (tl + 1) * 128],
                                        cens.t[:, ts(kc, 8)], start=(kc == 0), stop=(kc == 7))
                    p.post(p.fin(inst, "pe"), reads=[xf, cens], writes=[PC])
                SEC("pe", n_aff)

                def n_arg(v, tl=tl):
                    p.pre(v, "dve", reads=[PC], writes=[affb, smax, sidx, eidf, maskg])
                    p.fin(v.tensor_copy(affb.t[:, :], PC.t[:, 0:8]), "dve")
                    p.fin(v.max(smax.t[:, :], affb.t[:, :]), "dve")
                    p.fin(v.max_index(sidx.t[:, :], smax.t[:, :], affb.t[:, :]), "dve")
                    p.fin(v.tensor_copy(eidf.t[:, tl:tl + 1], sidx.t[:, 0:1]), "dve")
                    for e in range(E):
                        t = p.fin(v.tensor_single_scalar(maskg.t[:, tl * 8 + e: tl * 8 + e + 1],
                                                         eidf.t[:, tl:tl + 1], float(e), OP.is_equal), "dve")
                    p.post(t, reads=[PC], writes=[affb, smax, sidx, eidf, maskg])
                SEC("dve", n_arg)

                def n_alp(a, tl=tl):
                    p.pre(a, "act", reads=[smax], writes=[alphag])
                    t = p.fin(a.activation(alphag.t[:, tl:tl + 1], smax.t[:, 0:1], AF.Sigmoid), "act")
                    p.post(t, reads=[smax], writes=[alphag])
                SEC("act", n_alp)

                def n_gth(g, tl=tl):
                    p.pre(g, "pool", reads=[sidx], writes=[gtok, btok])
                    tok = p.dmatok(g.indirect_dma_start(
                        out=gtok.t[:, :], out_offset=None, in_=lng_h[:, :],
                        in_offset=bass.IndirectOffsetOnAxis(sidx.t[:, 0:1], 0)), "qg")
                    tok = p.dmatok(g.indirect_dma_start(
                        out=btok.t[:, :], out_offset=None, in_=lnb_h[:, :],
                        in_offset=bass.IndirectOffsetOnAxis(sidx.t[:, 0:1], 0)), "qg")
                    p.post(tok, reads=[sidx], writes=[gtok, btok])
                SEC("pool", n_gth)

                def n_ln(v, tl=tl):
                    p.pre(v, "dve", reads=[xing], writes=[mv])
                    p.fin(v.bn_stats(mv.t[:, 0:6], xing.t[:, tl * D: tl * D + 512]), "dve")
                    p.fin(v.bn_stats(mv.t[:, 6:12], xing.t[:, tl * D + 512: (tl + 1) * D]), "dve")
                    t = p.fin(v.bn_aggr(mv.t[:, 12:14], mv.t[:, 0:12]), "dve")
                    p.post(t, reads=[xing], writes=[mv])
                SEC("dve", n_ln)

                def n_lr(a):
                    p.pre(a, "act", reads=[mv], writes=[mv])
                    p.fin(a.activation(mv.t[:, 14:15], mv.t[:, 13:14], AF.Ln, bias=epsv.t[:, 0:1]), "act")
                    t = p.fin(a.activation(mv.t[:, 15:16], mv.t[:, 14:15], AF.Exp, scale=epsv.t[:, 2:3]), "act")
                    p.post(t, writes=[mv])
                SEC("act", n_lr)

                def n_xln(v, tl=tl):
                    b = tl % 2
                    p.pre(v, "dve", reads=[xing, mv, gtok, btok], writes=[sc1, xlt[b]])
                    p.fin(v.tensor_scalar(sc1.t[:, :], xing.t[:, ts(tl, D)], mv.t[:, 12:13],
                                          mv.t[:, 15:16], op0=OP.subtract, op1=OP.mult), "dve")
                    p.fin(v.tensor_mul(sc1.t[:, :], sc1.t[:, :], gtok.t[:, :]), "dve")
                    t = p.fin(v.tensor_add(xlt[b].t[:, :], sc1.t[:, :], btok.t[:, :]), "dve")
                    p.post(t, reads=[xing, mv, gtok, btok], writes=[sc1, xlt[b]])
                SEC("dve", n_xln)

                def n_xw(g, tl=tl, g_=g_):
                    b = tl % 2
                    DMA(g, ("qhx" if tl % 2 == 0 else "qhxb"), Dxln,
                        scr_xln[g_ * 512 + tl * 128: g_ * 512 + (tl + 1) * 128, :],
                        xlt[b], xlt[b].t[:, :])
                    if "xln" in TAPS:
                        DMA(g, "qo", None, TAPS["xln"][g_ * 512 + tl * 128: g_ * 512 + (tl + 1) * 128, :],
                            xlt[b], xlt[b].t[:, :])
                    if "eid" in TAPS:
                        DMA(g, "qo", None, TAPS["eid"][g_ * 4 + tl, :, 0:1], eidf, eidf.t[:, tl:tl + 1],
                            allow_slow_non_contiguous=True)
                        DMA(g, "qo", None, TAPS["eid"][g_ * 4 + tl, :, 1:2], alphag, alphag.t[:, tl:tl + 1],
                            allow_slow_non_contiguous=True)
                SEC("pool", n_xw)

            def n_lhop(s, g_=g_):
                for pc in range(8):
                    TDMA(s, "hxf", xf, xf.t[:, ts(pc, 512)], Dxln, scr_xln[grows, ts(pc, 128)])
            SEC("sp", n_lhop)
            p.endburst("hxf", Dxln)
            xf.w = ("hxf", p.cnt["hxf"]); xf.rs = []

            def n_acz(v):
                p.pre(v, "dve", writes=[accg])
                t = p.fin(v.memset(accg.t[:, :], 0.0), "dve")
                p.post(t, writes=[accg])
            SEC("dve", n_acz)

            for e in range(E):
                for fc in range(8):
                    wb = w1c[fc % 2]

                    def n_w1(g, e=e, fc=fc, wb=wb):
                        for kc in range(8):
                            DMA(g, ("qA" if fc % 2 == 0 else "qB"), wb, wb.t[:, ts(kc, 512)],
                                None, w1_h[e, ts(kc, 128), ts(fc, 512)])
                        wb.w = (("qA" if fc % 2 == 0 else "qB"), p.cnt["qA" if fc % 2 == 0 else "qB"]); wb.rs = []
                    SEC("pool", n_w1)

                    for ms in range(4):
                        PT = [PA, PBp][ms % 2]

                        def n_f1(t, fc=fc, ms=ms, wb=wb, PT=PT):
                            p.pre(t, "pe", reads=[wb, xf], writes=[PT])
                            for kc in range(8):
                                inst = t.matmul(PT.t[:, 0:512],
                                                wb.t[:, kc * 512 + ms * 128: kc * 512 + (ms + 1) * 128],
                                                xf.t[:, ts(kc, 512)], start=(kc == 0), stop=(kc == 7))
                            p.post(p.fin(inst, "pe"), reads=[wb, xf], writes=[PT])
                        SEC("pe", n_f1)

                        def n_r1(a, e=e, fc=fc, ms=ms, PT=PT):
                            p.pre(a, "act", reads=[PT, b1r], writes=[hgrp])
                            t = p.fin(a.activation(hgrp.t[:, (fc * 4 + ms) * 512: (fc * 4 + ms + 1) * 512],
                                                   PT.t[:, 0:512], AF.Relu,
                                                   bias=b1r.t[:, e * 32 + fc * 4 + ms: e * 32 + fc * 4 + ms + 1]), "act")
                            p.post(t, reads=[PT, b1r], writes=[hgrp])
                        SEC("act", n_r1)

                PT2 = [[(PG1, slice(0, 512)), (PG1, slice(512, 1024))],
                       [(PG2, slice(0, 512)), (PG2, slice(512, 1024))],
                       [(PC, slice(0, 512)), (PD, slice(0, 512))],
                       [(PA, slice(0, 512)), (PBp, slice(0, 512))]]
                for fc in range(8):
                    wb = w2c[fc % 2]

                    def n_w2(g, e=e, fc=fc, wb=wb):
                        for kc4 in range(4):
                            DMA(g, ("qC" if fc % 2 == 0 else "qD"), wb, wb.t[:, ts(kc4, D)],
                                None, w2_h[e, fc * 512 + kc4 * 128: fc * 512 + (kc4 + 1) * 128, :])
                        wb.w = (("qC" if fc % 2 == 0 else "qD"), p.cnt["qC" if fc % 2 == 0 else "qD"]); wb.rs = []
                    SEC("pool", n_w2)

                    for tl in range(4):
                        def n_f2(t, e=e, fc=fc, tl=tl, wb=wb):
                            pts = PT2[tl]
                            rd = [wb, hgrp, rsel, b2p]
                            p.pre(t, "pe", reads=rd, writes=[pts[0][0], pts[1][0]])
                            for nch, (PT, reg) in enumerate(pts):
                                if fc == 0:
                                    be = 32 * (e % 4)
                                    bc = (e // 4) * D
                                    t.matmul(PT.t[:, reg], rsel.t[:, ts(e % 4, 128)],
                                             b2p.t[:, bc + nch * 512: bc + (nch + 1) * 512],
                                             start=True, stop=False)
                                for kc4 in range(4):
                                    inst = t.matmul(PT.t[:, reg],
                                                    hgrp.t[:, (fc * 4 + kc4) * 512 + tl * 128: (fc * 4 + kc4) * 512 + (tl + 1) * 128],
                                                    wb.t[:, kc4 * D + nch * 512: kc4 * D + (nch + 1) * 512],
                                                    start=False, stop=(fc == 7 and kc4 == 3))
                                tk = p.fin(inst, "pe")
                                p.post(tk, reads=rd, writes=[PT])
                        SEC("pe", n_f2)

                for tl in range(4):
                    def n_acc(v, e=e, tl=tl):
                        pts = PT2[tl]
                        p.pre(v, "dve", reads=[pts[0][0], pts[1][0], maskg], writes=[accg])
                        for nch, (PT, reg) in enumerate(pts):
                            t = p.fin(v.scalar_tensor_tensor(
                                accg.t[:, tl * D + nch * 512: tl * D + (nch + 1) * 512],
                                PT.t[:, reg], maskg.t[:, tl * 8 + e: tl * 8 + e + 1],
                                accg.t[:, tl * D + nch * 512: tl * D + (nch + 1) * 512],
                                op0=OP.mult, op1=OP.add), "dve")
                        p.post(t, reads=[pts[0][0], pts[1][0], maskg], writes=[accg])
                    SEC("dve", n_acc)

            for tl in range(4):
                b = tl % 2

                def n_h1l(s, g_=g_, tl=tl, b=b):
                    DMA(s, "qx", h1t[0], h1t[0].t[:, :],
                        Dh1, scr_h1[g_ * 512 + tl * 128: g_ * 512 + (tl + 1) * 128, :], ek="sp")
                SEC("sp", n_h1l)

                def n_fbo(v, tl=tl, b=b):
                    p.pre(v, "dve", reads=[accg, alphag, xing], writes=[of32f, mv])
                    p.fin(v.scalar_tensor_tensor(of32f.t[:, 0:1024], accg.t[:, ts(tl, D)],
                                                 alphag.t[:, tl:tl + 1], xing.t[:, ts(tl, D)],
                                                 op0=OP.mult, op1=OP.add), "dve")
                    p.fin(v.bn_stats(mv.t[:, 0:6], of32f.t[:, 0:512]), "dve")
                    p.fin(v.bn_stats(mv.t[:, 6:12], of32f.t[:, 512:1024]), "dve")
                    t = p.fin(v.bn_aggr(mv.t[:, 12:14], mv.t[:, 0:12]), "dve")
                    p.post(t, reads=[accg, alphag, xing], writes=[of32f, mv])
                SEC("dve", n_fbo)

                def n_fr(a):
                    p.pre(a, "act", reads=[mv], writes=[mv])
                    p.fin(a.activation(mv.t[:, 14:15], mv.t[:, 13:14], AF.Ln, bias=epsv.t[:, 0:1]), "act")
                    t = p.fin(a.activation(mv.t[:, 15:16], mv.t[:, 14:15], AF.Exp, scale=epsv.t[:, 2:3]), "act")
                    p.post(t, writes=[mv])
                SEC("act", n_fr)

                def n_out(v, tl=tl, b=b):
                    p.pre(v, "dve", reads=[of32f, mv, h1t[0]], writes=[sc1])
                    p.fin(v.tensor_scalar(sc1.t[:, :], of32f.t[:, 0:1024], mv.t[:, 12:13],
                                          mv.t[:, 15:16], op0=OP.subtract, op1=OP.mult), "dve")
                    t = p.fin(v.tensor_add(sc1.t[:, :], sc1.t[:, :], h1t[0].t[:, :]), "dve")
                    p.post(t, reads=[of32f, mv, h1t[0]], writes=[sc1])
                SEC("dve", n_out)

                def n_odma(g, g_=g_, tl=tl):
                    DMA(g, "qo", None, out_h[g_ * 512 + tl * 128: g_ * 512 + (tl + 1) * 128, :],
                        sc1, sc1.t[:, :])
                SEC("pool", n_odma)

        def fino(g):
            g.wait_ge(p.sem["qo"], p.cnt["qo"])
            g.wait_ge(p.sem["qg"], p.cnt["qg"])
        SEC("pool", fino)

    return nc



def _consts(NSC):
    NT = NSC * SCT
    i32t8 = np.zeros((128, 256), np.float32)
    for c in range(256):
        i32t8[(c % 32)::32, c] = 0.0
    pr = np.arange(128)
    cc = np.arange(256)
    i32t8 = (pr[:, None] % 32 == cc[None, :] % 32).astype(np.float32)
    oneskm = np.ones((128, 128), np.float32)
    m1km = -np.ones((128, 128), np.float32)
    colsel = np.zeros((128, 64), np.float32)
    colsel[:, 0] = 1.0
    colsel[:, 32] = 1.0 / 256.0
    statsel = np.zeros((128, 288), np.float32)
    for blkk in range(3):
        for pc in range(8):
            statsel[:, blkk * 96 + pc * 12 + blkk * 4 + pc // 2] = 1.0 / 256.0
    g2m = np.zeros((128, CPS * 256), np.float32)
    for m in range(CPS):
        for c in range(256):
            tchunk, tt_ = c // 32, c % 32
            if tchunk > m:
                g2m[:, m * 256 + c] = 1.0
            elif tchunk == m:
                g2m[pr % 32 <= tt_, m * 256 + c] = 1.0
    nmask = np.zeros((128, 256), np.float32)
    for c in range(256):
        nmask[pr % 32 < (c % 32), c] = 1.0
    i32sp = np.zeros((128, 256), np.float32)
    for n in range(8):
        for c in range(32):
            i32sp[32 * (n % 4) + c, 32 * n + c] = 1.0
    gatesel = np.zeros((128, 12 * 128), np.float32)
    for r in range(12):
        gatesel[r, r * 128:(r + 1) * 128] = 1.0
    rsel = np.zeros((128, 512), np.float32)
    rselm = np.zeros((128, 512), np.float32)
    for h in range(4):
        rsel[32 * h, h * 128:(h + 1) * 128] = 1.0
        rselm[32 * h, h * 128:(h + 1) * 128] = -1.0
    return dict(i32t8=i32t8, i32sp=i32sp, oneskm=oneskm, m1km=m1km, colsel=colsel,
                statsel=statsel, g2m=g2m, nmask=nmask, gatesel=gatesel, rsel=rsel, rselm=rselm,
                rself=rsel.copy())


def make_inputs(core_inputs, NSC=8):
    gi = core_inputs
    NT = NSC * SCT
    B = gi["L0_attn_hidden_states"].shape[0]
    L = gi["L0_attn_hidden_states"].shape[1]
    halves = L // NT
    n_cores = B * halves
    c = _consts(NSC)
    wb = gi["L0_attn_Wb"]; wid = gi["L0_attn_Wid"]
    wbid = np.zeros((D, 256), np.float32)
    for h in range(4):
        wbid[:, 32 * h] = wb[:, h]
        wbid[:, 128 + 32 * h] = wid[:, h]
    bidp = np.zeros((128, 1), np.float32)
    alphap = np.zeros((128, 1), np.float32)
    for h in range(4):
        bidp[32 * h, 0] = gi["L0_attn_bid"][h]
        alphap[32 * h, 0] = gi["L0_attn_alpha_id"][h]
    conv = {}
    for nn, key in (("q", "L0_attn_conv_q"), ("k", "L0_attn_conv_k"), ("v", "L0_attn_conv_v")):
        conv[nn] = np.ascontiguousarray(
            gi[key].reshape(8, 128, 4).transpose(1, 0, 2).reshape(128, 32))
    firs = np.ascontiguousarray(
        gi["L0_attn_fir_short"].reshape(1024, 3).reshape(8, 128, 3).transpose(1, 0, 2).reshape(128, 24))
    firl = np.ascontiguousarray(
        gi["L0_attn_fir_long"].reshape(1024, 31).reshape(8, 128, 31).transpose(1, 0, 2).reshape(128, 248))
    wr1 = gi["L0_attn_Wr1"]
    mean_rows = [1024 + 8 * t + h for t in range(3) for h in range(4)]
    std_rows = [1024 + 8 * t + 4 + h for t in range(3) for h in range(4)]
    wr1m = np.ascontiguousarray(wr1[0:1024])
    wr1sm = np.ascontiguousarray(wr1[mean_rows])
    wr1ss = np.ascontiguousarray(wr1[std_rows])
    br1r = np.ascontiguousarray(gi["L0_attn_br1"].reshape(16, 128).T)
    br2r = gi["L0_attn_br2"].reshape(12, 1)
    ltg12 = np.repeat(gi["L0_attn_log_tau_group"], 6).reshape(12, 1)
    onwr = np.ascontiguousarray(gi["L0_attn_o_norm_w"].reshape(2, 128).T)
    cent = np.ascontiguousarray(gi["L0_ffn_centroids"].T)
    b1r = np.zeros((128, E * 32), np.float32)
    for e in range(E):
        b1r[:, e * 32:(e + 1) * 32] = gi["L0_ffn_b1"][e].reshape(32, 128).T
    b2p = np.zeros((E * 32, D), np.float32)
    for e in range(E):
        b2p[32 * e] = gi["L0_ffn_b2"][e]

    common = dict(
        wq=gi["L0_attn_Wq"], wk=gi["L0_attn_Wk"], wv=gi["L0_attn_Wv"],
        wo=gi["L0_attn_Wo"], wbid=wbid, bidp=bidp, alphap=alphap,
        convq=conv["q"], convk=conv["k"], convv=conv["v"], firs=firs, firl=firl,
        wr1m=wr1m, wr1sm=wr1sm, wr1ss=wr1ss, br1r=br1r, wr2=gi["L0_attn_Wr2"],
        br2r=br2r, ltg12=ltg12, onwr=onwr,
        cent=cent, lng=gi["L0_ffn_ln_g"], lnb=gi["L0_ffn_ln_b"],
        w1=gi["L0_ffn_w1"], w2=gi["L0_ffn_w2"], b1r=b1r, b2p=b2p, **c)
    common["epsv"] = np.tile(np.array([1e-5, 1e-6, -0.5, 0.0], np.float32), (128, 1))
    common = {k: np.ascontiguousarray(v, dtype=np.float32) for k, v in common.items()}

    in_maps = []
    x = gi["L0_attn_hidden_states"]
    for core in range(n_cores):
        b, hf = core // halves, core % halves
        t0 = hf * NT
        if t0 == 0:
            xslice = np.concatenate([np.zeros((HALO, D), np.float32), x[b, 0:NT]], axis=0)
        else:
            xslice = x[b, t0 - HALO: t0 + NT]
        m = dict(common)
        m["x"] = np.ascontiguousarray(xslice, dtype=np.float32)
        m["hflag"] = np.full((128, 1), float(hf > 0), np.float32)
        m["hflagn"] = np.full((128, 1), float(hf == 0), np.float32)
        in_maps.append(m)
    return in_maps, n_cores


def kernel(**inputs):
    NSC = 8
    NT = NSC * SCT
    in_maps, n_cores = make_inputs(inputs, NSC)
    nc = build_program(NSC=NSC, pairs=n_cores // 2)
    res = run_bass_kernel_spmd(nc, in_maps, core_ids=list(range(n_cores)))
    B = inputs["L0_attn_hidden_states"].shape[0]
    L = inputs["L0_attn_hidden_states"].shape[1]
    halves = L // NT
    out = np.zeros((B, L, D), np.float32)
    for core in range(n_cores):
        b, hf = core // halves, core % halves
        out[b, hf * NT:(hf + 1) * NT] = np.asarray(res.results[core]["out"]).reshape(NT, D)
    return out


# version 05
# speedup vs baseline: 1.0085x; 1.0085x over previous
import contextlib
import numpy as np

import concourse.bass as bass
import concourse.mybir as mybir
from concourse.alu_op_type import AluOpType as OP
from concourse.bass_utils import run_bass_kernel_spmd

F32 = mybir.dt.float32
BF16 = mybir.dt.bfloat16
U32 = mybir.dt.uint32
AF = mybir.ActivationFunctionType

H, D, CH, CPS = 4, 1024, 32, 8
SCT = CH * CPS
HALO = 128
E, FF = 8, 4096

ENG = {"pe": "tensor", "dve": "vector", "act": "scalar", "pool": "gpsimd", "sp": "sync"}


class Bf:
    __slots__ = ("t", "w", "rs")

    def __init__(self, t):
        self.t = t
        self.w = None
        self.rs = []


class PB:
    def __init__(self, nc, es, names):
        self.nc = nc
        self.sem = {n: es.enter_context(nc.semaphore(n)) for n in names}
        self.cnt = {n: 0 for n in names}
        self.hw = {}

    def _w(self, eng, ek, tok):
        if tok is None:
            return
        s, v = tok
        if self.hw.get((ek, s), 0) < v:
            eng.wait_ge(self.sem[s], v)
            self.hw[(ek, s)] = v

    def pre(self, eng, ek, reads=(), writes=(), toks=(), skip_sem=None):
        for b in reads:
            if b.w is not None and b.w[0] != skip_sem:
                self._w(eng, ek, b.w)
        for b in writes:
            if b.w is not None and b.w[0] != skip_sem:
                self._w(eng, ek, b.w)
            for r in b.rs:
                self._w(eng, ek, r)
        for t in toks:
            self._w(eng, ek, t)

    def post(self, tok, reads=(), writes=()):
        for b in writes:
            b.w = tok
            b.rs = []
        for b in reads:
            b.rs.append(tok)
        return tok

    def fin(self, inst, s):
        c = self.cnt[s]
        inst.then_inc(self.sem[s], 1)
        if c > 0:
            inst._wait_ge(self.sem[s], c)
        self.cnt[s] = c + 1
        return (s, c + 1)

    def dmatok(self, inst, s):
        inst.then_inc(self.sem[s], 16)
        self.cnt[s] += 16
        return (s, self.cnt[s])

    def endburst(self, qs, *bufs):
        t = (qs, self.cnt[qs])
        for b in bufs:
            b.rs = [r for r in b.rs if r[0] != qs] + [t]
            if b.w is not None and b.w[0] == qs:
                b.w = t


def ts(i, n):
    return slice(i * n, (i + 1) * n)


def build_program(NSC=8, taps=(), pairs=4, stop_after=None):
    tful_holder = [None]
    NT = NSC * SCT
    NTH = NT + HALO
    NG = NT // 512
    nc = bass.Bass("TRN2", target_bir_lowering=False, debug=False)
    tapset = set(taps)

    dt = lambda n, s, d=F32, k="ExternalInput": nc.dram_tensor(n, s, d, kind=k)
    x_h = dt("x", [NTH, D])
    w_h = {n: dt(n, [D, D]) for n in ("wq", "wk", "wv", "wo")}
    wbid_h = dt("wbid", [D, 256])
    bidp_h = dt("bidp", [128, 1]); alphap_h = dt("alphap", [128, 1])
    conv_h = {n: dt("conv" + n, [128, 32]) for n in "qkv"}
    firs_h = dt("firs", [128, 24]); firl_h = dt("firl", [128, 248])
    wr1m_h = dt("wr1m", [D, 2048]); wr1sm_h = dt("wr1sm", [12, 2048]); wr1ss_h = dt("wr1ss", [12, 2048])
    br1_h = dt("br1r", [128, 16]); wr2_h = dt("wr2", [2048, 12]); br2_h = dt("br2r", [12, 1])
    ltg12_h = dt("ltg12", [12, 1])
    onw_h = dt("onwr", [128, 2])
    cen_h = dt("cent", [D, E])
    lng_h = dt("lng", [E, D]); lnb_h = dt("lnb", [E, D])
    w1_h = dt("w1", [E, D, FF]); w2_h = dt("w2", [E, FF, D])
    b1_h = dt("b1r", [128, E * 32]); b2p_h = dt("b2p", [E * 32, D])
    i32_h = dt("i32t8", [128, 256])
    i32sp_h = dt("i32sp", [128, 256])
    ones_h = dt("oneskm", [128, 128]); m1_h = dt("m1km", [128, 128])
    colsel_h = dt("colsel", [128, 64])
    statsel_h = dt("statsel", [128, 288])
    g2m_h = dt("g2m", [128, CPS * 256])
    nmask_h = dt("nmask", [128, 256])
    gatesel_h = dt("gatesel", [128, 12 * 128])
    rsel_h = dt("rsel", [128, 512])
    rselm_h = dt("rselm", [128, 512])
    rself_h = dt("rself", [128, 512])
    hflag_h = dt("hflag", [128, 1])
    hflagn_h = dt("hflagn", [128, 1])
    epsv_h = dt("epsv", [128, 4])
    out_h = dt("out", [NT, D], F32, "ExternalOutput")
    TAPS = {}
    tapdefs = {"q": [D, NT], "k": [D, NT], "v": [D, NT], "kb": [D, NT],
               "beta": [128, NT], "idsc": [128, NT], "xn": [NTH, D],
               "delta": [NT, D], "send": [128, 2048], "fs": [D, NT], "fl": [D, NT],
               "smean": [12, NT], "sstd": [12, NT], "p": [16, NT], "o": [D, NT],
               "h1": [NT, D], "xin": [NT, D], "eid": [NT // 128, 128, 2],
               "xln": [NT, D], "hmid": [2048, NT],
               "g1t": [128, CPS * 256], "g2t": [128, CPS * 256], "n1t": [128, 256],
               "tful": [128, 256], "u2": [128, CPS * 256], "u2p": [128, 2048],
               "zt": [128, CPS * 256]}
    for tn in tapset:
        TAPS[tn] = dt("tap_" + tn, tapdefs[tn], F32, "ExternalOutput")

    scr_xn = dt("scr_xn", [NTH, D], BF16, "Internal")
    scr_k = dt("scr_k", [D, NT], BF16, "Internal")
    scr_v = dt("scr_v", [D, NT], BF16, "Internal")
    scr_d = dt("scr_d", [NT, D], BF16, "Internal")
    scr_xin = dt("scr_xin", [NT, D], BF16, "Internal")
    scr_h1 = dt("scr_h1", [NT, D], BF16, "Internal")
    scr_xln = dt("scr_xln", [NT, D], BF16, "Internal")
    scr_u2 = dt("scr_u2", [4, 256, 256], BF16, "Internal")
    scr_lg = dt("scr_lg", [16, 256], BF16, "Internal")
    scr_p = dt("scr_p", [256, 128], BF16, "Internal")
    scr_w1 = dt("scr_w1", [E, D, FF], BF16, "Internal")
    scr_w2 = dt("scr_w2", [E, FF, D], BF16, "Internal")
    cc_i = nc.dram_tensor("ccs_in", [1024, 256], F32)
    cc_o = nc.dram_tensor("ccs_out", [1024, 256], F32)

    es = contextlib.ExitStack()
    with es:
        ats = contextlib.ExitStack()

        def sb(n, s, d=BF16):
            return Bf(es.enter_context(nc.sbuf_tensor(n, s, d)))

        def sbA(n, s, d=BF16):
            return Bf(ats.enter_context(nc.sbuf_tensor(n, s, d)))

        def psum(n, s):
            return Bf(es.enter_context(nc.psum_tensor(n, s, F32)))

        epsv = sb("epsv_s", [128, 4], F32)
        oneskm = sb("ones_s", [128, 128])
        mv = sb("mv_s", [128, 16], F32)
        sc1 = sb("sc1_s", [128, 1024], F32)
        h1t = [sb("h1t0", [128, D])]
        wkr = sbA("wkr_s", [128, 8 * D])
        wvr = sbA("wvr_s", [128, 8 * D])
        wob = [sbA(f"wob{i}_s", [128, D]) for i in range(2)]
        wpb = wob
        wbid = sbA("wbid_s", [128, 8 * 256])
        wr1 = sbA("wr1_s", [128, 8 * 2048])
        wr1sm = sbA("wr1sm_s", [12, 2048]); wr1ss = sbA("wr1ss_s", [12, 2048])
        wr2 = sbA("wr2_s", [128, 16 * 12])
        convw = sbA("convw_s", [128, 96], F32)
        firw = sbA("firw_s", [128, 272], F32)
        br1 = sbA("br1_s", [128, 16], F32); br2 = sbA("br2_s", [12, 1], F32)
        tau12 = sbA("tau12_s", [12, 1], F32); it12 = sbA("it12_s", [12, 1], F32)
        ltg = sbA("ltg12_s", [12, 1], F32)
        onw = sbA("onw_s", [128, 2], F32)
        bidp = sbA("bidp_s", [128, 1], F32); alphap = sbA("alphap_s", [128, 1], F32)
        siga = sbA("siga_s", [128, 1], F32)
        hflag = sbA("hflag_s", [128, 1], F32)
        hflagn = sbA("hflagn_s", [128, 1], F32)
        i32t = sbA("i32t_s", [128, 256])
        i32sp = sbA("i32sp_s", [128, 256])
        oneskmf = sbA("onesf_s", [128, 128], F32)
        m1km = sbA("m1_s", [128, 128])
        colsel = sbA("colsel_s", [128, 64])
        statsel = sbA("statsel_s", [128, 288])
        g2m = sbA("g2m_s", [128, CPS * 256])
        nmask = sbA("nmask_s", [128, 256])
        gatesel = sbA("gatesel_s", [128, 12 * 128])
        rsel = sbA("rsel_s", [128, 512])
        rselm = sbA("rselm_s", [128, 512])
        rself = sbA("rself_s", [128, 512], F32)
        xnf = sbA("xnf_s", [128, 8 * 384])
        praw = sbA("praw_s", [128, 320])
        qf = sbA("qf_s", [128, 8 * 256]); kf = sbA("kf_s", [128, 8 * 256])
        vf = sbA("vf_s", [128, 8 * 288])
        kbn = sbA("kbn_s", [128, 8 * 256])
        sq = sbA("sq_s", [128, 8 * 256])
        betap = sbA("betap_s", [128, 256])
        idsc = sbA("idsc_s", [128, 256])
        nrm = sbA("nrm_s", [128, 512], F32)
        fac = sbA("fac_s", [128, 288], F32)
        fsf = sbA("fsf_s", [128, 8 * 256]); flf = sbA("flf_s", [128, 8 * 256])
        g1t = sbA("g1t_s", [128, CPS * 256]); g2t = sbA("g2t_s", [128, CPS * 256])
        nst = {k: sbA(f"n{k}_s", [128, 256]) for k in ["1t", "1", "2", "2t", "4", "4t", "8", "8t", "16", "16t"]}
        zsb = sbA("zsb_s", [128, 256])
        rt = [sbA(f"rt{i}_s", [128, 256]) for i in range(2)] + [zsb]
        u2il = sbA("u2il_s", [128, CPS * 256])
        u2pl = sbA("u2pl_s", [128, 2 * 1024])
        vtm = sbA("vtm_s", [128, 2 * 1024])
        ktm = sbA("ktm_s", [128, 2 * 1024])
        sfp = sbA("sfp_s", [128, 2048], F32)
        sbfb = sbA("sbf_s", [128, 2048])
        dfm = sbA("dfm_s", [128, 8 * 256])
        sstat = sbA("sstat_s", [12, 512])
        hmid = sbA("hmid_s", [128, 8 * 256])
        lgp = sbA("lgp_s", [16, 256])
        ph_ = sbA("ph_s", [128, 32])
        ptf = sbA("ptf_s", [128, 16], F32)
        mxg = sbA("mxg_s", [128, 4], F32)
        dif = sbA("dif_s", [128, 16], F32)
        exg = sbA("exg_s", [128, 16], F32)
        sumr = sbA("sumr_s", [128, 4], F32)
        ptmb = sbA("ptmb_s", [128, 128])
        pfm = sbA("pfm_s", [128, 256])
        of32 = sbA("of32_s", [128, 512], F32)
        gbh = sbA("gbh_s", [128, 1024])
        tmx = [sbA("tmx0_s", [128, 256], F32)]
        off = sbA("off_s", [128, 8 * 256])
        xres = [sbA("xres0", [128, D])]
        xint = [sbA("xint0", [128, D])]

        PA = psum("PA", [128, 512]); PBp = psum("PB", [128, 512])
        PC = psum("PC", [128, 512]); PD = psum("PD", [128, 512])
        PG1 = psum("PG1", [128, 1024]); PG2 = psum("PG2", [128, 1024])

        Dxn = Bf(scr_xn); Dk = Bf(scr_k); Dv = Bf(scr_v); Dd = Bf(scr_d)
        Dxin = Bf(scr_xin); Dh1 = Bf(scr_h1); Dxln = Bf(scr_xln)
        Dlg = Bf(scr_lg); Dp = Bf(scr_p); Dci = Bf(cc_i); Dco = Bf(cc_o)
        Du2 = Bf(scr_u2); Dw1 = Bf(scr_w1); Dw2 = Bf(scr_w2)

        p = PB(nc, es, ["pe", "dve", "act", "pool", "qw", "qx", "qxn", "qkv", "qd", "qhx", "qlg", "qg", "qo", "cc", "qp0", "qp1", "qb0", "qb1", "qA", "qB", "qC", "qD", "qxr", "hxnf", "htm", "hd", "hlg", "hp", "hxf", "qhxb", "qu2", "qud"])
        blk = es.enter_context(nc.Block())
        SEC = lambda ek, fn: getattr(blk, ENG[ek])(fn)

        def DMA(g, qs, outb, outap, inb, inap, ek="pool", **kw):
            wr = [outb] if outb is not None else []
            rd = [inb] if inb is not None else []
            p.pre(g, ek, reads=rd, writes=wr, skip_sem=qs)
            tok = p.dmatok(g.dma_start(out=outap, in_=inap, **kw), qs)
            if qs == "qo":
                p._w(g, ek, tok)
            p.post(tok, reads=rd, writes=wr)
            return tok

        def TDMA(s, qs, outb, outap, inb, inap, ek="sp"):
            wr = [outb] if outb is not None else []
            rd = [inb] if inb is not None else []
            p.pre(s, ek, reads=rd, writes=wr, skip_sem=qs)
            tok = p.dmatok(s.dma_start(out=outap, in_=inap, transpose=True), qs)
            p.post(tok, reads=rd, writes=wr)
            return tok

        def ph0(g):
            ld = lambda o_ap, i_ap: p.dmatok(g.dma_start(out=o_ap, in_=i_ap), "qw")
            for kc in range(8):
                ld(wr1.t[:, ts(kc, 2048)], wr1m_h[ts(kc, 128), :])
                ld(wbid.t[:, ts(kc, 256)], wbid_h[ts(kc, 128), :])
            for kc in range(16):
                ld(wr2.t[:, ts(kc, 12)], wr2_h[ts(kc, 128), :])
            for kc in range(8):
                ld(wkr.t[:, ts(kc, D)], w_h["wk"][ts(kc, 128), :])
                ld(wvr.t[:, ts(kc, D)], w_h["wv"][ts(kc, 128), :])
            ld(wr1sm.t[:, :], wr1sm_h[:, :]); ld(wr1ss.t[:, :], wr1ss_h[:, :])
            ld(convw.t[:, 0:32], conv_h["q"][:, :]); ld(convw.t[:, 32:64], conv_h["k"][:, :])
            ld(convw.t[:, 64:96], conv_h["v"][:, :])
            ld(firw.t[:, 0:24], firs_h[:, :]); ld(firw.t[:, 24:272], firl_h[:, :])
            ld(br1.t[:, :], br1_h[:, :]); ld(br2.t[:, :], br2_h[:, :])
            ld(ltg.t[:, :], ltg12_h[:, :]); ld(onw.t[:, :], onw_h[:, :])
            ld(bidp.t[:, :], bidp_h[:, :]); ld(alphap.t[:, :], alphap_h[:, :])
            ld(hflag.t[:, :], hflag_h[:, :]); ld(hflagn.t[:, :], hflagn_h[:, :])
            ld(i32t.t[:, :], i32_h[:, :])
            ld(epsv.t[:, :], epsv_h[:, :])
            ld(i32sp.t[:, :], i32sp_h[:, :])
            ld(oneskm.t[:, :], ones_h[:, :]); ld(oneskmf.t[:, :], ones_h[:, :])
            ld(m1km.t[:, :], m1_h[:, :]); ld(colsel.t[:, :], colsel_h[:, :])
            ld(statsel.t[:, :], statsel_h[:, :]); ld(g2m.t[:, :], g2m_h[:, :])
            ld(nmask.t[:, :], nmask_h[:, :]); ld(gatesel.t[:, :], gatesel_h[:, :])
            ld(rsel.t[:, :], rsel_h[:, :]); ld(rselm.t[:, :], rselm_h[:, :])
            ld(rself.t[:, :], rself_h[:, :])
            tok = ("qw", p.cnt["qw"])
            for b in [wbid, wr1, wr1sm, wr1ss, wr2, convw, firw, br1,
                      br2, ltg, onw, bidp, alphap, hflag, i32t, oneskm, oneskmf, m1km,
                      colsel, statsel, g2m, nmask, gatesel, rsel, rselm, rself, epsv, hflagn, i32sp,
                      wkr, wvr]:
                b.w = tok
        SEC("pool", ph0)

        def ph0b(a):
            p.pre(a, "act", reads=[alphap, ltg], writes=[siga, tau12])
            t1 = p.fin(a.activation(siga.t[:, :], alphap.t[:, :], AF.Sigmoid), "act")
            p.post(t1, reads=[alphap], writes=[siga])
            t2 = p.fin(a.activation(tau12.t[:, :], ltg.t[:, :], AF.Exp), "act")
            p.post(t2, reads=[ltg], writes=[tau12])
        SEC("act", ph0b)

        def ph0c(v):
            p.pre(v, "dve", reads=[tau12], writes=[it12, sfp, sbfb, lgp, ptmb])
            t = p.fin(v.reciprocal(it12.t[:, :], tau12.t[:, :]), "dve")
            p.post(t, reads=[tau12], writes=[it12])
            p.post(p.fin(v.memset(sfp.t[:, :], 0.0), "dve"), writes=[sfp])
            p.post(p.fin(v.memset(sbfb.t[:, :], 0.0), "dve"), writes=[sbfb])
            p.post(p.fin(v.memset(lgp.t[:, :], 0.0), "dve"), writes=[lgp])
            p.post(p.fin(v.memset(ptmb.t[:, :], 0.0), "dve"), writes=[ptmb])
        SEC("dve", ph0c)

        xt = [xres[0]]
        xnt = [h1t[0]]
        ntile = NTH // 128
        for i in range(ntile):
            b = 0

            def f1(g, i=i, b=b):
                DMA(g, "qxr", xt[b], xt[b].t[:, :], None, x_h[ts(i, 128), :])
            SEC("pool", f1)

            def f2(v, i=i, b=b):
                p.pre(v, "dve", reads=[xt[b]], writes=[mv])
                p.fin(v.bn_stats(mv.t[:, 0:6], xt[b].t[:, 0:512]), "dve")
                p.fin(v.bn_stats(mv.t[:, 6:12], xt[b].t[:, 512:1024]), "dve")
                t = p.fin(v.bn_aggr(mv.t[:, 12:14], mv.t[:, 0:12]), "dve")
                p.post(t, reads=[xt[b]], writes=[mv])
            SEC("dve", f2)

            def f3(a, i=i):
                p.pre(a, "act", reads=[mv], writes=[mv])
                p.fin(a.activation(mv.t[:, 14:15], mv.t[:, 13:14], AF.Ln, bias=epsv.t[:, 0:1]), "act")
                t = p.fin(a.activation(mv.t[:, 15:16], mv.t[:, 14:15], AF.Exp, scale=epsv.t[:, 2:3]), "act")
                p.post(t, writes=[mv])
            SEC("act", f3)

            def f4(v, i=i, b=b):
                p.pre(v, "dve", reads=[xt[b], mv], writes=[xnt[b]])
                t = p.fin(v.tensor_scalar(xnt[b].t[:, :], xt[b].t[:, :], mv.t[:, 12:13],
                                          mv.t[:, 15:16], op0=OP.subtract, op1=OP.mult), "dve")
                p.post(t, reads=[xt[b], mv], writes=[xnt[b]])
            SEC("dve", f4)

            def f5(g, i=i, b=b):
                DMA(g, "qxn", Dxn, scr_xn[ts(i, 128), :], xnt[b], xnt[b].t[:, :])
                if "xn" in TAPS:
                    DMA(g, "qo", None, TAPS["xn"][ts(i, 128), :], xnt[b], xnt[b].t[:, :])
            SEC("pool", f5)

        def early_out():
            snapx = dict(p.cnt)

            def _fx(e, ek):
                for s_, v_ in snapx.items():
                    if v_ > 0:
                        p._w(e, ek, (s_, v_))
            for ek_ in ("pe", "dve", "act", "pool", "sp"):
                SEC(ek_, lambda e, ek_=ek_: _fx(e, ek_))
            ats.close()

            def eo(v):
                p.pre(v, "dve", writes=[sc1])
                t = p.fin(v.memset(sc1.t[:, :], 0.0), "dve")
                p.post(t, writes=[sc1])
            SEC("dve", eo)

            def eo2(g):
                for i in range(NT // 128):
                    DMA(g, "qo", None, out_h[ts(i, 128), :], sc1, sc1.t[:, :])
                g.wait_ge(p.sem["qo"], p.cnt["qo"])
            SEC("pool", eo2)

        if stop_after == "p1":
            early_out()
            return nc

        snap1 = dict(p.cnt)

        def _fence1(e, ek):
            for s_, v_ in snap1.items():
                if v_ > 0:
                    p._w(e, ek, (s_, v_))
        for ek_ in ("pe", "dve", "act", "pool", "sp"):
            SEC(ek_, lambda e, ek_=ek_: _fence1(e, ek_))

        def wcast(g):
            for e in range(E):
                DMA(g, "qA", Dw1, scr_w1[e, :, :], None, w1_h[e, :, :])
            for e in range(E):
                DMA(g, "qC", Dw2, scr_w2[e, :, :], None, w2_h[e, :, :])
        SEC("pool", wcast)
        p.endburst("qA", Dw1)
        p.endburst("qC", Dw2)

        def sctk(sc):
            return slice(sc * SCT, (sc + 1) * SCT)

        def emit_sc(sc, full, upto=None):
            def a_hop(s):
                for pc in range(8):
                    TDMA(s, "hxnf", xnf if pc == 7 else None, xnf.t[:, ts(pc, 384)],
                         Dxn, scr_xn[sc * SCT: sc * SCT + 384, ts(pc, 128)])
            SEC("sp", a_hop)
            p.endburst("hxnf", Dxn)
            xnf.w = ("hxnf", p.cnt["hxnf"])
            xnf.rs = []

            projs = [("q", None, 0, qf, 256, 61), ("k", wkr, 32, kf, 256, 61),
                     ("v", wvr, 64, vf, 288, 29)]
            if not full:
                projs = projs[1:]
            PSUM8 = [(PA, slice(0, 320)), (PBp, slice(0, 320)), (PC, slice(0, 320)),
                     (PD, slice(0, 320)), (PG1, slice(0, 320)), (PG1, slice(512, 832)),
                     (PG2, slice(0, 320)), (PG2, slice(512, 832))]
            for pn, wt, cwo, dstf, wide, base in projs:
                if wt is None:
                    for kc in range(8):
                        wb = wpb[kc % 2]

                        def fwl(g, kc=kc, wb=wb):
                            DMA(g, ("qp0" if kc % 2 == 0 else "qp1"), wb, wb.t[:, :],
                                None, w_h["wq"][ts(kc, 128), :])
                        SEC("pool", fwl)

                        def fmm(t, kc=kc, wb=wb):
                            wrs_ = [PA, PBp, PC, PD, PG1, PG2] if kc == 0 else []
                            p.pre(t, "pe", reads=[xnf, wb], writes=wrs_)
                            for pc in range(8):
                                PT, reg = PSUM8[pc]
                                inst = t.matmul(PT.t[:, reg], wb.t[:, ts(pc, 128)],
                                                xnf.t[:, kc * 384 + 64: (kc + 1) * 384],
                                                start=(kc == 0), stop=(kc == 7))
                            tk = p.fin(inst, "pe")
                            p.post(tk, reads=[xnf, wb],
                                   writes=[PA, PBp, PC, PD, PG1, PG2] if kc == 7 else [])
                        SEC("pe", fmm)
                else:
                    def fmm(t, wt=wt):
                        p.pre(t, "pe", reads=[xnf, wt], writes=[PA, PBp, PC, PD, PG1, PG2])
                        for kc in range(8):
                            for pc in range(8):
                                PT, reg = PSUM8[pc]
                                inst = t.matmul(PT.t[:, reg], wt.t[:, kc * D + pc * 128: kc * D + (pc + 1) * 128],
                                                xnf.t[:, kc * 384 + 64: (kc + 1) * 384],
                                                start=(kc == 0), stop=(kc == 7))
                        tk = p.fin(inst, "pe")
                        p.post(tk, reads=[xnf, wt], writes=[PA, PBp, PC, PD, PG1, PG2])
                    SEC("pe", fmm)

                for pc in range(8):
                    PT, reg = PSUM8[pc]

                    def fev(a, PT=PT, reg=reg):
                        p.pre(a, "act", reads=[PT], writes=[praw])
                        t = p.fin(a.copy(praw.t[:, :], PT.t[:, reg]), "act")
                        p.post(t, reads=[PT], writes=[praw])
                    SEC("act", fev)

                    def fcv(v, pn=pn, pc=pc, cwo=cwo, wide=wide, base=base):
                        p.pre(v, "dve", reads=[praw, convw], writes=[fac])
                        p.fin(v.tensor_scalar_mul(fac.t[:, 0:wide], praw.t[:, base: base + wide],
                                                  convw.t[:, cwo + pc * 4: cwo + pc * 4 + 1]), "dve")
                        for j in range(1, 4):
                            t = p.fin(v.scalar_tensor_tensor(
                                fac.t[:, 0:wide], praw.t[:, base + j: base + j + wide],
                                convw.t[:, cwo + pc * 4 + j: cwo + pc * 4 + j + 1],
                                fac.t[:, 0:wide], op0=OP.mult, op1=OP.add), "dve")
                        p.post(t, reads=[praw, convw], writes=[fac])
                    SEC("dve", fcv)

                    def fsl(a, pc=pc, dstf=dstf, wide=wide):
                        p.pre(a, "act", reads=[fac], writes=[dstf])
                        t = p.fin(a.activation(dstf.t[:, pc * wide: (pc + 1) * wide],
                                               fac.t[:, 0:wide], AF.Silu), "act")
                        p.post(t, reads=[fac], writes=[dstf])
                    SEC("act", fsl)

            if upto == "proj":
                return
            def cmm(t):
                p.pre(t, "pe", reads=[xnf, wbid], writes=[PC])
                for msl in range(2 if full else 1):
                    for kc in range(8):
                        inst = t.matmul(PC.t[:, msl * 256: msl * 256 + 256],
                                        wbid.t[:, kc * 256 + msl * 128: kc * 256 + msl * 128 + 128],
                                        xnf.t[:, kc * 384 + 128: (kc + 1) * 384],
                                        start=(kc == 0), stop=(kc == 7))
                p.post(p.fin(inst, "pe"), reads=[xnf, wbid], writes=[PC])
            SEC("pe", cmm)

            def cact(a):
                p.pre(a, "act", reads=[PC, bidp], writes=[betap, idsc])
                t = p.fin(a.activation(betap.t[:, 0:256], PC.t[:, 0:256], AF.Sigmoid), "act")
                if full:
                    t = p.fin(a.activation(idsc.t[:, 0:256], PC.t[:, 256:512], AF.Sigmoid,
                                           bias=bidp.t[:, 0:1]), "act")
                p.post(t, reads=[PC, bidp], writes=[betap] + ([idsc] if full else []))
            SEC("act", cact)

            if full:
                def cid(v):
                    p.pre(v, "dve", reads=[siga], writes=[idsc])
                    t = p.fin(v.tensor_scalar(idsc.t[:, 0:256], idsc.t[:, 0:256],
                                              siga.t[:, 0:1], 0.06, op0=OP.mult, op1=OP.add), "dve")
                    p.post(t, reads=[siga], writes=[idsc])
                SEC("dve", cid)

            if upto == "beta":
                return
            nlist = [("q", qf, 0, PC), ("k", kf, 256, PD)] if full else [("k", kf, 256, PD)]
            for pn, dstf, nro, PT in nlist:
                def dsq(v, dstf=dstf):
                    p.pre(v, "dve", reads=[dstf], writes=[sq])
                    for pc in range(8):
                        t = p.fin(v.tensor_mul(sq.t[:, ts(pc, 256)], dstf.t[:, ts(pc, 256)],
                                               dstf.t[:, ts(pc, 256)]), "dve")
                    p.post(t, reads=[dstf], writes=[sq])
                SEC("dve", dsq)

                def dmm(t, PT=PT):
                    p.pre(t, "pe", reads=[sq, colsel], writes=[PT])
                    for h in range(H):
                        for kci in range(2):
                            pc = 2 * h + kci
                            inst = t.matmul(PT.t[32 * h:32 * h + 32, 0:256], colsel.t[:, 0:32],
                                            sq.t[:, ts(pc, 256)], start=(kci == 0), stop=(kci == 1),
                                            tile_position=(0, 32 * h))
                    p.post(p.fin(inst, "pe"), reads=[sq, colsel], writes=[PT])
                SEC("pe", dmm)

                def drs(a, nro=nro, PT=PT):
                    p.pre(a, "act", reads=[PT], writes=[nrm])
                    for h in range(H):
                        p.fin(a.activation(nrm.t[32 * h:32 * h + 32, nro: nro + 256],
                                           PT.t[32 * h:32 * h + 32, 0:256], AF.Ln,
                                           bias=epsv.t[32 * h:32 * h + 32, 1:2]), "act")
                        t = p.fin(a.activation(nrm.t[32 * h:32 * h + 32, nro: nro + 256],
                                               nrm.t[32 * h:32 * h + 32, nro: nro + 256],
                                               AF.Exp, scale=epsv.t[32 * h:32 * h + 32, 2:3]), "act")
                    p.post(t, reads=[PT], writes=[nrm])
                SEC("act", drs)

                for h in range(H):
                    PTb = [PC, PD][h % 2]

                    def dbc(t, h=h, nro=nro, PTb=PTb, PT=PT):
                        p.pre(t, "pe", reads=[nrm, rself], writes=[PTb])
                        inst = t.matmul(PTb.t[:, 256:512], rself.t[:, ts(h, 128)],
                                        nrm.t[:, nro: nro + 256], start=True, stop=True)
                        p.post(p.fin(inst, "pe"), reads=[nrm, rself], writes=[PTb])
                    SEC("pe", dbc)

                    def dap(v, h=h, dstf=dstf, PTb=PTb):
                        p.pre(v, "dve", reads=[PTb], writes=[dstf])
                        for kci in range(2):
                            pc = 2 * h + kci
                            t = p.fin(v.tensor_mul(dstf.t[:, ts(pc, 256)], dstf.t[:, ts(pc, 256)],
                                                   PTb.t[:, 256:512]), "dve")
                        p.post(t, reads=[PTb], writes=[dstf])
                    SEC("dve", dap)

            for h in range(H):
                PTb = [PC, PD][h % 2]

                def dbb(t, h=h, PTb=PTb):
                    p.pre(t, "pe", reads=[betap, rselm], writes=[PTb])
                    inst = t.matmul(PTb.t[:, 256:512], rselm.t[:, ts(h, 128)],
                                    betap.t[:, 0:256], start=True, stop=True)
                    p.post(p.fin(inst, "pe"), reads=[betap, rselm], writes=[PTb])
                SEC("pe", dbb)

                def dkb(v, h=h, PTb=PTb):
                    p.pre(v, "dve", reads=[PTb, kf, vf], writes=[kbn, sq])
                    for kci in range(2):
                        pc = 2 * h + kci
                        p.fin(v.tensor_mul(kbn.t[:, ts(pc, 256)], kf.t[:, ts(pc, 256)],
                                           PTb.t[:, 256:512]), "dve")
                        t = p.fin(v.scalar_tensor_tensor(sq.t[:, ts(pc, 256)],
                                                         vf.t[:, pc * 288 + 32: (pc + 1) * 288],
                                                         -1.0, PTb.t[:, 256:512],
                                                         op0=OP.mult, op1=OP.mult), "dve")
                    p.post(t, reads=[PTb, kf, vf], writes=[kbn, sq])
                SEC("dve", dkb)

            if upto == "norm":
                return
            def est(g):
                for pc in range(8):
                    DMA(g, "qkv", Dk, scr_k[ts(pc, 128), sctk(sc)], kf, kf.t[:, ts(pc, 256)])
                    DMA(g, "qkv", Dv, scr_v[ts(pc, 128), sctk(sc)], sq, sq.t[:, ts(pc, 256)])
                p.endburst("qkv", Dk, Dv, kf, sq)
                if full:
                    for tn, bsrc in (("q", qf), ("k", kf), ("v", None), ("kb", kbn)):
                        if tn in TAPS:
                            for pc in range(8):
                                srcap = (vf.t[:, pc * 288 + 32: (pc + 1) * 288] if tn == "v"
                                         else bsrc.t[:, ts(pc, 256)])
                                DMA(g, "qo", None, TAPS[tn][ts(pc, 128), sctk(sc)],
                                    bsrc if tn != "v" else vf, srcap)
                    if "beta" in TAPS:
                        DMA(g, "qo", None, TAPS["beta"][:, sctk(sc)], betap, betap.t[:, 0:256])
                    if "idsc" in TAPS:
                        DMA(g, "qo", None, TAPS["idsc"][:, sctk(sc)], idsc, idsc.t[:, 0:256])
            SEC("pool", est)

            if upto == "est":
                return
            def fhp(s):
                for hh in range(2):
                    for h in range(H):
                        TDMA(s, "htm", ktm, ktm.t[:, (hh * 4 + h) * 256: (hh * 4 + h + 1) * 256],
                             Dk, scr_k[ts(h, 256), sc * SCT + hh * 128: sc * SCT + (hh + 1) * 128])
                        TDMA(s, "htm", vtm, vtm.t[:, (hh * 4 + h) * 256: (hh * 4 + h + 1) * 256],
                             Dv, scr_v[ts(h, 256), sc * SCT + hh * 128: sc * SCT + (hh + 1) * 128])
            SEC("sp", fhp)
            p.endburst("htm", Dk, Dv)
            ktm.w = vtm.w = ("htm", p.cnt["htm"])
            ktm.rs = []; vtm.rs = []

        def emit_scan(sc, full, upto=None):
            def gmm(t):
                p.pre(t, "pe", reads=[kf, kbn], writes=[PC])
                for n in range(CPS):
                    for h in range(H):
                        for kci in range(2):
                            pc = 2 * h + kci
                            inst = t.matmul(PC.t[32 * h:32 * h + 32, ts(n, 32)],
                                            kf.t[:, pc * 256 + n * 32: pc * 256 + (n + 1) * 32],
                                            kbn.t[:, pc * 256 + n * 32: pc * 256 + (n + 1) * 32],
                                            start=(kci == 0), stop=(kci == 1),
                                            tile_position=(0, 32 * h))
                p.post(p.fin(inst, "pe"), reads=[kf, kbn], writes=[PC])
            SEC("pe", gmm)

            def gev(v):
                p.pre(v, "dve", reads=[PC, nmask], writes=[nst["1t"], nst["1"]])
                t = p.fin(v.tensor_mul(nst["1t"].t[:, :], PC.t[:, 0:256], nmask.t[:, :]), "dve")
                p.post(t, reads=[PC, nmask], writes=[nst["1t"]])
                t = p.fin(v.transpose(nst["1"].t[:, :], nst["1t"].t[:, :]), "dve")
                p.post(t, reads=[nst["1t"]], writes=[nst["1"]])
            SEC("dve", gev)

            if upto == "g":
                return
            for dst, lhs, rhs in (("2", "1t", "1"), ("4", "2t", "2"), ("8", "4t", "4"), ("16", "8t", "8")):
                def hmm(t, dst=dst, lhs=lhs, rhs=rhs):
                    p.pre(t, "pe", reads=[nst[lhs], nst[rhs]], writes=[PD])
                    for n in range(CPS):
                        for h in range(H):
                            inst = t.matmul(PD.t[32 * h:32 * h + 32, ts(n, 32)],
                                            nst[lhs].t[32 * h:32 * h + 32, ts(n, 32)],
                                            nst[rhs].t[32 * h:32 * h + 32, ts(n, 32)],
                                            start=True, stop=True, tile_position=(32 * h, 32 * h))
                    p.post(p.fin(inst, "pe"), reads=[nst[lhs], nst[rhs]], writes=[PD])
                SEC("pe", hmm)

                def hev(v, dst=dst):
                    p.pre(v, "dve", reads=[PD], writes=[nst[dst], nst[dst + "t"]])
                    t = p.fin(v.tensor_copy(nst[dst].t[:, :], PD.t[:, 0:256]), "dve")
                    p.post(t, reads=[PD], writes=[nst[dst]])
                    t = p.fin(v.transpose(nst[dst + "t"].t[:, :], nst[dst].t[:, :]), "dve")
                    p.post(t, reads=[nst[dst]], writes=[nst[dst + "t"]])
                SEC("dve", hev)

            def hch0(v):
                p.pre(v, "dve", reads=[nst["16t"], i32t], writes=[rt[0]])
                t = p.fin(v.tensor_add(rt[0].t[:, :], nst["16t"].t[:, :], i32t.t[:, :]), "dve")
                p.post(t, reads=[nst["16t"], i32t], writes=[rt[0]])
            SEC("dve", hch0)
            rord = [rt[0], rt[1], rt[2], rt[0], rt[1]]
            for li, kk in enumerate(["8", "4", "2", "1"]):
                src, dst = rord[li], rord[li + 1]

                def hcm(t, kk=kk, src=src):
                    p.pre(t, "pe", reads=[nst[kk], src], writes=[PD])
                    for n in range(CPS):
                        for h in range(H):
                            inst = t.matmul(PD.t[32 * h:32 * h + 32, ts(n, 32)],
                                            nst[kk].t[32 * h:32 * h + 32, ts(n, 32)],
                                            src.t[32 * h:32 * h + 32, ts(n, 32)],
                                            start=True, stop=True, tile_position=(32 * h, 32 * h))
                    p.post(p.fin(inst, "pe"), reads=[nst[kk], src], writes=[PD])
                SEC("pe", hcm)

                def hce(v, src=src, dst=dst):
                    p.pre(v, "dve", reads=[PD, src], writes=[dst])
                    t = p.fin(v.tensor_add(dst.t[:, :], PD.t[:, 0:256], src.t[:, :]), "dve")
                    p.post(t, reads=[PD, src], writes=[dst])
                SEC("dve", hce)
            tful = rord[4]
            tful_holder[0] = tful

            if upto == "t":
                return
            glist = [("g1", kbn, g1t, False, PG1)] + ([("g2", qf, g2t, True, PG2)] if full else [])
            for gn, rhsf, dstt, masked, PT in glist:
                for m in range(CPS):
                    reg = slice(0, 256) if m % 2 == 0 else slice(512, 768)

                    def imm(t, gn=gn, rhsf=rhsf, m=m, PT=PT, reg=reg):
                        p.pre(t, "pe", reads=[kf, rhsf], writes=[PT])
                        for h in range(H):
                            for kci in range(2):
                                pc = 2 * h + kci
                                inst = t.matmul(PT.t[32 * h:32 * h + 32, reg],
                                                kf.t[:, pc * 256 + m * 32: pc * 256 + (m + 1) * 32],
                                                rhsf.t[:, ts(pc, 256)],
                                                start=(kci == 0), stop=(kci == 1),
                                                tile_position=(0, 32 * h))
                        p.post(p.fin(inst, "pe"), reads=[kf, rhsf], writes=[PT])
                    SEC("pe", imm)

                    def iev(v, gn=gn, dstt=dstt, masked=masked, m=m, PT=PT, reg=reg):
                        p.pre(v, "dve", reads=[PT, g2m], writes=[dstt])
                        if masked:
                            t = p.fin(v.tensor_mul(dstt.t[:, ts(m, 256)], PT.t[:, reg],
                                                   g2m.t[:, ts(m, 256)]), "dve")
                        else:
                            t = p.fin(v.tensor_copy(dstt.t[:, ts(m, 256)], PT.t[:, reg]), "dve")
                        p.post(t, reads=[PT], writes=[dstt])
                    SEC("dve", iev)

            if upto == "i":
                return
            jstage = upto[1:] if (upto and upto.startswith("j") and len(upto) > 1) else None
            for n in range(CPS):
                hh, r = n // 4, n % 4

                def jz(t, n=n, hh=hh, r=r):
                    p.pre(t, "pe", reads=[kbn, sbfb, g1t, u2il, i32sp, vtm], writes=[PA])
                    for h in range(H):
                        mms = []
                        for kci in range(2):
                            pc = 2 * h + kci
                            mms.append((kbn.t[:, pc * 256 + n * 32: pc * 256 + (n + 1) * 32],
                                        sbfb.t[:, ts(pc, 256)], (0, 32 * h)))
                        for m in range(n):
                            mms.append((g1t.t[32 * h:32 * h + 32, m * 256 + n * 32: m * 256 + (n + 1) * 32],
                                        u2il.t[32 * h:32 * h + 32, ts(m, 256)], (32 * h, 32 * h)))
                        mms.append((i32sp.t[:, ts(n, 32)],
                                    vtm.t[:, (hh * 4 + h) * 256: (hh * 4 + h + 1) * 256],
                                    (0, 32 * h)))
                        for j, (lh, rh, tp) in enumerate(mms):
                            inst = t.matmul(PA.t[32 * h:32 * h + 32, 0:256], lh, rh,
                                            start=(j == 0), stop=(j == len(mms) - 1),
                                            tile_position=tp)
                    p.post(p.fin(inst, "pe"), reads=[kbn, sbfb, g1t, u2il, i32sp, vtm], writes=[PA])
                SEC("pe", jz)
                if jstage == "1":
                    if n == CPS - 1:
                        def jfl(v):
                            p.pre(v, "dve", reads=[PA], writes=[zsb])
                            t = p.fin(v.tensor_copy(zsb.t[:, :], PA.t[:, 0:256]), "dve")
                            p.post(t, reads=[PA], writes=[zsb])
                        SEC("dve", jfl)
                    continue

                def jze(v, n=n):
                    p.pre(v, "dve", reads=[PA], writes=[zsb])
                    t = p.fin(v.tensor_copy(zsb.t[:, :], PA.t[:, 0:256]), "dve")
                    p.post(t, reads=[PA], writes=[zsb])
                SEC("dve", jze)
                if full and sc == 0 and "zt" in TAPS:
                    def ztap(g, n=n):
                        DMA(g, "qo", None, TAPS["zt"][:, ts(n, 256)], zsb, zsb.t[:, :])
                    SEC("pool", ztap)

                def ju2(t, n=n, hh=hh, r=r):
                    p.pre(t, "pe", reads=[tful, zsb], writes=[PBp])
                    for h in range(H):
                        inst = t.matmul(PBp.t[32 * h:32 * h + 32, 0:256],
                                        tful.t[32 * h:32 * h + 32, ts(n, 32)],
                                        zsb.t[32 * h:32 * h + 32, :],
                                        start=True, stop=True, tile_position=(32 * h, 32 * h))
                    tk1 = p.fin(inst, "pe")
                    p.post(tk1, reads=[tful, zsb], writes=[PBp])
                SEC("pe", ju2)

                def ju2e(v, n=n, hh=hh):
                    p.pre(v, "dve", reads=[PBp], writes=[u2il])
                    t = p.fin(v.tensor_copy(u2il.t[:, ts(n, 256)], PBp.t[:, 0:256]), "dve")
                    p.post(t, reads=[PBp], writes=[u2il])
                SEC("dve", ju2e)

                def ju2w(g, n=n):
                    DMA(g, "qud", Du2, scr_u2[:, 32 * n:32 * n + 32, :], u2il,
                        u2il.t[:, ts(n, 256)])
                SEC("pool", ju2w)

                if n == CPS - 1:
                    def ju2r(s):
                        p.endburst("qud", Du2)
                        for hh2 in range(2):
                            for h in range(H):
                                DMA(s, "qu2", u2pl,
                                    u2pl.t[:, hh2 * 1024 + h * 256: hh2 * 1024 + (h + 1) * 256],
                                    Du2, scr_u2[h, hh2 * 128:(hh2 + 1) * 128, :], ek="sp")
                        p.endburst("qu2", Du2)
                        u2pl.w = ("qu2", p.cnt["qu2"])
                        u2pl.rs = []
                    SEC("sp", ju2r)

            if upto is not None and upto.startswith("j"):
                return
            if full and sc == 0:
                def scantap(g):
                    for tn, bsrc in (("g1t", g1t), ("g2t", g2t), ("n1t", nst["1t"]),
                                     ("tful", tful_holder[0]), ("u2", u2il), ("u2p", u2pl)):
                        if tn in TAPS:
                            DMA(g, "qo", None, TAPS[tn][:, :], bsrc, bsrc.t[:, :])
                SEC("pool", scantap)

            if full:
                for tsl in range(2):
                    for h in range(H):
                        PT = [PC, PD][(tsl * H + h) % 2]

                        def lmm(t, tsl=tsl, h=h, PT=PT):
                            p.pre(t, "pe", reads=[qf, sbfb, g2t, u2il], writes=[PT])
                            mms = []
                            for kci in range(2):
                                pc = 2 * h + kci
                                mms.append((qf.t[:, pc * 256 + tsl * 128: pc * 256 + (tsl + 1) * 128],
                                            sbfb.t[:, ts(pc, 256)], None))
                            for m in range(CPS):
                                mms.append((g2t.t[32 * h:32 * h + 32, m * 256 + tsl * 128: m * 256 + (tsl + 1) * 128],
                                            u2il.t[32 * h:32 * h + 32, ts(m, 256)], (32 * h, 0)))
                            for j, (lh, rh, tp) in enumerate(mms):
                                inst = t.matmul(PT.t[:, 0:256], lh, rh, start=(j == 0),
                                                stop=(j == len(mms) - 1), tile_position=tp)
                            p.post(p.fin(inst, "pe"), reads=[qf, sbfb, g2t, u2il], writes=[PT])
                        SEC("pe", lmm)

                        def lev(v, tsl=tsl, h=h, PT=PT):
                            p.pre(v, "dve", reads=[PT], writes=[hmid])
                            t = p.fin(v.tensor_copy(hmid.t[:, tsl * 1024 + h * 256: tsl * 1024 + (h + 1) * 256],
                                                    PT.t[:, 0:256]), "dve")
                            p.post(t, reads=[PT], writes=[hmid])
                        SEC("dve", lev)

                def ldma(g):
                    for tsl in range(2):
                        DMA(g, "qd", Dd, scr_d[sc * SCT + tsl * 128: sc * SCT + (tsl + 1) * 128, :],
                            hmid, hmid.t[:, ts(tsl, 1024)])
                        if "delta" in TAPS:
                            DMA(g, "qo", None, TAPS["delta"][sc * SCT + tsl * 128: sc * SCT + (tsl + 1) * 128, :],
                                hmid, hmid.t[:, ts(tsl, 1024)])
                SEC("pool", ldma)
                p.endburst("qd", Dd, hmid)

                def lhop(s):
                    for pc in range(8):
                        TDMA(s, "hd", dfm, dfm.t[:, ts(pc, 256)],
                             Dd, scr_d[sctk(sc), ts(pc, 128)])
                SEC("sp", lhop)
                p.endburst("hd", Dd)
                dfm.w = ("hd", p.cnt["hd"]); dfm.rs = []

            for h in range(H):
                for dks in range(2):
                    PT = [PC, PD][(h * 2 + dks) % 2]

                    def kmm(t, h=h, dks=dks, PT=PT):
                        p.pre(t, "pe", reads=[ktm, u2pl], writes=[PT])
                        for hh in range(2):
                            inst = t.matmul(PT.t[:, 0:256],
                                            ktm.t[:, (hh * 4 + h) * 256 + dks * 128: (hh * 4 + h) * 256 + (dks + 1) * 128],
                                            u2pl.t[:, hh * 1024 + h * 256: hh * 1024 + (h + 1) * 256],
                                            start=(hh == 0), stop=(hh == 1))
                        p.post(p.fin(inst, "pe"), reads=[ktm, u2pl], writes=[PT])
                    SEC("pe", kmm)

                    def kev(v, h=h, dks=dks, PT=PT):
                        blkc = (2 * h + dks) * 256
                        p.pre(v, "dve", reads=[PT], writes=[sfp, sbfb])
                        p.fin(v.tensor_add(sfp.t[:, blkc:blkc + 256], sfp.t[:, blkc:blkc + 256],
                                           PT.t[:, 0:256]), "dve")
                        t = p.fin(v.tensor_copy(sbfb.t[:, blkc:blkc + 256], sfp.t[:, blkc:blkc + 256]), "dve")
                        p.post(t, reads=[PT], writes=[sfp, sbfb])
                    SEC("dve", kev)

        def emit_tail(sc, upto=None):
            sco = sctk(sc)
            for pc in range(8):
                def tfir(v, pc=pc):
                    p.pre(v, "dve", reads=[vf, firw], writes=[fac, fsf, sq])
                    p.fin(v.tensor_scalar_mul(fac.t[:, 0:256], vf.t[:, pc * 288 + 30: pc * 288 + 286],
                                              firw.t[:, pc * 3: pc * 3 + 1]), "dve")
                    for j in range(1, 3):
                        p.fin(v.scalar_tensor_tensor(fac.t[:, 0:256],
                                                     vf.t[:, pc * 288 + 30 + j: pc * 288 + 286 + j],
                                                     firw.t[:, pc * 3 + j: pc * 3 + j + 1],
                                                     fac.t[:, 0:256], op0=OP.mult, op1=OP.add), "dve")
                    t = p.fin(v.tensor_copy(fsf.t[:, ts(pc, 256)], fac.t[:, 0:256]), "dve")
                    p.post(t, reads=[vf, firw], writes=[fsf])
                    t = p.fin(v.tensor_mul(sq.t[:, ts(pc, 256)], fac.t[:, 0:256], fac.t[:, 0:256]), "dve")
                    p.post(t, reads=[], writes=[sq, fac])
                SEC("dve", tfir)
            def tst1(t):
                p.pre(t, "pe", reads=[fsf, sq, statsel], writes=[PC, PD])
                for pc in range(8):
                    i1 = t.matmul(PC.t[0:12, 0:256], statsel.t[:, 0 + pc * 12: 0 + (pc + 1) * 12],
                                  fsf.t[:, ts(pc, 256)], start=(pc == 0), stop=False)
                tk = p.fin(i1, "pe"); p.post(tk, reads=[fsf, statsel], writes=[PC])
                for pc in range(8):
                    i2 = t.matmul(PD.t[0:12, 0:256], statsel.t[:, 0 + pc * 12: 0 + (pc + 1) * 12],
                                  sq.t[:, ts(pc, 256)], start=(pc == 0), stop=False)
                tk = p.fin(i2, "pe"); p.post(tk, reads=[sq, statsel], writes=[PD])
            SEC("pe", tst1)

            for pc in range(8):
                def tfl(v, pc=pc):
                    p.pre(v, "dve", reads=[vf, firw], writes=[fac])
                    p.fin(v.tensor_scalar_mul(fac.t[:, 0:256], vf.t[:, pc * 288 + 2: pc * 288 + 258],
                                              firw.t[:, 24 + pc * 31: 24 + pc * 31 + 1]), "dve")
                    for j in range(1, 31):
                        t = p.fin(v.scalar_tensor_tensor(fac.t[:, 0:256],
                                                         vf.t[:, pc * 288 + 2 + j: pc * 288 + 258 + j],
                                                         firw.t[:, 24 + pc * 31 + j: 24 + pc * 31 + j + 1],
                                                         fac.t[:, 0:256], op0=OP.mult, op1=OP.add), "dve")
                    p.post(t, reads=[vf, firw], writes=[fac])
                SEC("dve", tfl)

                def tflm(v, pc=pc):
                    p.pre(v, "dve", reads=[fac], writes=[flf, sq])
                    t = p.fin(v.tensor_copy(flf.t[:, ts(pc, 256)], fac.t[:, 0:256]), "dve")
                    p.post(t, reads=[fac], writes=[flf])
                    t = p.fin(v.tensor_mul(sq.t[:, ts(pc, 256)], flf.t[:, ts(pc, 256)],
                                           flf.t[:, ts(pc, 256)]), "dve")
                    p.post(t, reads=[flf], writes=[sq])
                SEC("dve", tflm)

            def tst2(t):
                p.pre(t, "pe", reads=[flf, sq, statsel], writes=[PC, PD])
                for pc in range(8):
                    i1 = t.matmul(PC.t[0:12, 0:256], statsel.t[:, 96 + pc * 12: 96 + (pc + 1) * 12],
                                  flf.t[:, ts(pc, 256)], start=False, stop=False)
                tk = p.fin(i1, "pe"); p.post(tk, reads=[flf, statsel], writes=[PC])
                for pc in range(8):
                    i2 = t.matmul(PD.t[0:12, 0:256], statsel.t[:, 96 + pc * 12: 96 + (pc + 1) * 12],
                                  sq.t[:, ts(pc, 256)], start=False, stop=False)
                tk = p.fin(i2, "pe"); p.post(tk, reads=[sq, statsel], writes=[PD])
            SEC("pe", tst2)

            for pc in range(8):
                def tdsq(v, pc=pc):
                    p.pre(v, "dve", reads=[dfm], writes=[sq])
                    t = p.fin(v.tensor_mul(sq.t[:, ts(pc, 256)], dfm.t[:, ts(pc, 256)],
                                           dfm.t[:, ts(pc, 256)]), "dve")
                    p.post(t, reads=[dfm], writes=[sq])
                SEC("dve", tdsq)

            def tst3(t):
                p.pre(t, "pe", reads=[dfm, sq, statsel], writes=[PC, PD])
                for pc in range(8):
                    i1 = t.matmul(PC.t[0:12, 0:256], statsel.t[:, 192 + pc * 12: 192 + (pc + 1) * 12],
                                  dfm.t[:, ts(pc, 256)], start=False, stop=(pc == 7))
                tk = p.fin(i1, "pe"); p.post(tk, reads=[dfm, statsel], writes=[PC])
                for pc in range(8):
                    i2 = t.matmul(PD.t[0:12, 0:256], statsel.t[:, 192 + pc * 12: 192 + (pc + 1) * 12],
                                  sq.t[:, ts(pc, 256)], start=False, stop=(pc == 7))
                tk = p.fin(i2, "pe"); p.post(tk, reads=[sq, statsel], writes=[PD])
            SEC("pe", tst3)

            def tms(v):
                p.pre(v, "dve", reads=[PC, PD], writes=[nrm, sstat])
                p.fin(v.tensor_copy(nrm.t[0:12, 0:256], PC.t[0:12, 0:256]), "dve")
                p.fin(v.tensor_mul(nrm.t[0:12, 256:512], nrm.t[0:12, 0:256], nrm.t[0:12, 0:256]), "dve")
                p.fin(v.scalar_tensor_tensor(nrm.t[0:12, 256:512], PD.t[0:12, 0:256], 1.0,
                                             nrm.t[0:12, 256:512], op0=OP.mult, op1=OP.subtract), "dve")
                p.fin(v.tensor_scalar_max(nrm.t[0:12, 256:512], nrm.t[0:12, 256:512], 0.0), "dve")
                t = p.fin(v.tensor_copy(sstat.t[:, 0:256], nrm.t[0:12, 0:256]), "dve")
                p.post(t, reads=[PC, PD], writes=[nrm, sstat])
            SEC("dve", tms)

            def tsd(a):
                p.pre(a, "act", reads=[nrm], writes=[sstat])
                t = p.fin(a.activation(sstat.t[:, 256:512], nrm.t[0:12, 256:512], AF.Sqrt), "act")
                p.post(t, reads=[nrm], writes=[sstat])
            SEC("act", tsd)

            if "smean" in TAPS or "sstd" in TAPS:
                def tstap(g):
                    if "smean" in TAPS:
                        DMA(g, "qo", None, TAPS["smean"][:, sco], sstat, sstat.t[:, 0:256])
                    if "sstd" in TAPS:
                        DMA(g, "qo", None, TAPS["sstd"][:, sco], sstat, sstat.t[:, 256:512])
                SEC("pool", tstap)
            if "fs" in TAPS or "fl" in TAPS:
                def tftap(g):
                    for pc in range(8):
                        if "fs" in TAPS:
                            DMA(g, "qo", None, TAPS["fs"][ts(pc, 128), sco], fsf, fsf.t[:, ts(pc, 256)])
                        if "fl" in TAPS:
                            DMA(g, "qo", None, TAPS["fl"][ts(pc, 128), sco], flf, flf.t[:, ts(pc, 256)])
                SEC("pool", tftap)

            if upto == "stats":
                return
            for half in range(2):
                for fci in range(8):
                    fc = half * 8 + fci
                    PT = [PA, PBp][fci % 2]

                    def rmm(t, fc=fc, PT=PT):
                        p.pre(t, "pe", reads=[wr1, xnf, wr1sm, wr1ss, sstat], writes=[PT])
                        for kc in range(8):
                            t.matmul(PT.t[:, 0:256], wr1.t[:, kc * 2048 + fc * 128: kc * 2048 + (fc + 1) * 128],
                                     xnf.t[:, kc * 384 + 128: (kc + 1) * 384], start=(kc == 0), stop=False)
                        t.matmul(PT.t[:, 0:256], wr1sm.t[:, ts(fc, 128)], sstat.t[:, 0:256],
                                 start=False, stop=False)
                        inst = t.matmul(PT.t[:, 0:256], wr1ss.t[:, ts(fc, 128)], sstat.t[:, 256:512],
                                        start=False, stop=True)
                        p.post(p.fin(inst, "pe"), reads=[wr1, xnf, wr1sm, wr1ss, sstat], writes=[PT])
                    SEC("pe", rmm)

                    def rge(a, fc=fc, fci=fci, PT=PT):
                        p.pre(a, "act", reads=[PT, br1], writes=[hmid])
                        t = p.fin(a.activation(hmid.t[:, ts(fci, 256)], PT.t[:, 0:256], AF.Gelu,
                                               bias=br1.t[:, fc:fc + 1]), "act")
                        p.post(t, reads=[PT, br1], writes=[hmid])
                    SEC("act", rge)

                def rlg(t, half=half):
                    p.pre(t, "pe", reads=[wr2, hmid], writes=[PC])
                    for kci in range(8):
                        kc = half * 8 + kci
                        inst = t.matmul(PC.t[0:12, 0:256], wr2.t[:, ts(kc, 12)], hmid.t[:, ts(kci, 256)],
                                        start=(kc == 0), stop=(kc == 15))
                    p.post(p.fin(inst, "pe"), reads=[wr2, hmid], writes=[PC])
                SEC("pe", rlg)

            if upto == "rtr":
                return
            def rls(v):
                p.pre(v, "dve", reads=[PC, br2, it12], writes=[lgp])
                t = p.fin(v.tensor_scalar(lgp.t[0:12, :], PC.t[0:12, 0:256], br2.t[:, 0:1],
                                          it12.t[:, 0:1], op0=OP.add, op1=OP.mult), "dve")
                p.post(t, reads=[PC, br2, it12], writes=[lgp])
            SEC("dve", rls)

            if upto == "p1":
                return
            def rlw(g):
                DMA(g, "qlg", Dlg, scr_lg[:, :], lgp, lgp.t[:, :])
            SEC("pool", rlw)

            if upto == "p2a":
                return
            for hh in range(2):
                def rhp(s, hh=hh):
                    TDMA(s, "hlg", ph_, ph_.t[:, ts(hh, 16)], Dlg, scr_lg[:, ts(hh, 128)])
                SEC("sp", rhp)

                if upto == "p2":
                    continue
                def rsm(v, hh=hh):
                    p.pre(v, "dve", reads=[ph_], writes=[ptf, mxg, dif, exg])
                    p.fin(v.tensor_copy(ptf.t[:, 0:12], ph_.t[:, hh * 16: hh * 16 + 12]), "dve")
                    p.fin(v.tensor_max(mxg.t[:, :], ptf.t[:, 0:12:3], ptf.t[:, 1:12:3]), "dve")
                    p.fin(v.tensor_max(mxg.t[:, :], mxg.t[:, :], ptf.t[:, 2:12:3]), "dve")
                    mb = bass.AP(mxg.t, 0, [[4, 128], [1, 4], [0, 3]])
                    t = p.fin(v.tensor_sub(dif.t[:, 0:12], ptf.t[:, 0:12], mb), "dve")
                    p.post(t, reads=[ph_], writes=[ptf, mxg, dif])
                SEC("dve", rsm)

                if upto == "p3":
                    continue
                def rex(a, hh=hh):
                    p.pre(a, "act", reads=[dif], writes=[exg])
                    t = p.fin(a.activation(exg.t[:, 0:12], dif.t[:, 0:12], AF.Exp), "act")
                    p.post(t, reads=[dif], writes=[exg])
                SEC("act", rex)

                if upto == "p4":
                    continue
                def rp2(v, hh=hh):
                    p.pre(v, "dve", reads=[exg], writes=[sumr, ptmb])
                    p.fin(v.tensor_add(sumr.t[:, :], exg.t[:, 0:12:3], exg.t[:, 1:12:3]), "dve")
                    p.fin(v.tensor_add(sumr.t[:, :], sumr.t[:, :], exg.t[:, 2:12:3]), "dve")
                    p.fin(v.reciprocal(sumr.t[:, :], sumr.t[:, :]), "dve")
                    sb_ = bass.AP(sumr.t, 0, [[4, 128], [1, 4], [0, 3]])
                    p.fin(v.tensor_mul(dif.t[:, 0:12], exg.t[:, 0:12], sb_), "dve")
                    t = p.fin(v.tensor_scalar(ptmb.t[:, 0:12], dif.t[:, 0:12],
                                              1.0 - 3 * 0.025, 0.025, op0=OP.mult, op1=OP.add), "dve")
                    p.post(t, reads=[exg], writes=[sumr, ptmb, dif])
                SEC("dve", rp2)

                if upto == "p5":
                    continue
                def rpw(g, hh=hh):
                    DMA(g, "qlg", Dp, scr_p[ts(hh, 128), :], ptmb, ptmb.t[:, :])
                SEC("pool", rpw)

                if upto == "p6":
                    continue
                def rph(s, hh=hh):
                    TDMA(s, "hp", pfm, pfm.t[:, ts(hh, 128)], Dp, scr_p[ts(hh, 128), :])
                SEC("sp", rph)
            if "p" in TAPS:
                def ptap(g):
                    DMA(g, "qo", None, TAPS["p"][:, sco], pfm, pfm.t[0:16, :])
                SEC("pool", ptap)

            if upto == "p":
                return
            for h in range(H):
                def mbc(t, h=h):
                    p.pre(t, "pe", reads=[gatesel, pfm, rsel, idsc], writes=[PG2])
                    for gg in range(3):
                        inst = t.matmul(PG2.t[:, ts(gg, 256)], gatesel.t[:, ts(3 * h + gg, 128)],
                                        pfm.t[:, :], start=True, stop=True)
                    inst = t.matmul(PG2.t[:, 768:1024], rsel.t[:, ts(h, 128)],
                                    idsc.t[:, 0:256], start=True, stop=True)
                    p.post(p.fin(inst, "pe"), reads=[gatesel, pfm, rsel, idsc], writes=[PG2])
                SEC("pe", mbc)

                if upto == "m1":
                    continue
                def mge(v, h=h):
                    p.pre(v, "dve", reads=[PG2], writes=[gbh])
                    t = p.fin(v.tensor_copy(gbh.t[:, :], PG2.t[:, 0:1024]), "dve")
                    p.post(t, reads=[PG2], writes=[gbh])
                SEC("dve", mge)

                for kci in range(2):
                    pc = 2 * h + kci

                    def mmx(v, h=h, kci=kci, pc=pc):
                        p.pre(v, "dve", reads=[fsf, flf, dfm, vf, gbh], writes=[of32, tmx[0], sq])
                        oc = slice(kci * 256, (kci + 1) * 256)
                        p.fin(v.tensor_mul(of32.t[:, oc], fsf.t[:, ts(pc, 256)], gbh.t[:, 0:256]), "dve")
                        p.fin(v.tensor_mul(tmx[0].t[:, :], flf.t[:, ts(pc, 256)], gbh.t[:, 256:512]), "dve")
                        p.fin(v.tensor_add(of32.t[:, oc], of32.t[:, oc], tmx[0].t[:, :]), "dve")
                        p.fin(v.tensor_mul(tmx[0].t[:, :], dfm.t[:, ts(pc, 256)], gbh.t[:, 512:768]), "dve")
                        p.fin(v.tensor_add(of32.t[:, oc], of32.t[:, oc], tmx[0].t[:, :]), "dve")
                        p.fin(v.tensor_mul(tmx[0].t[:, :], vf.t[:, pc * 288 + 32: (pc + 1) * 288],
                                           gbh.t[:, 768:1024]), "dve")
                        p.fin(v.tensor_add(of32.t[:, oc], of32.t[:, oc], tmx[0].t[:, :]), "dve")
                        t = p.fin(v.tensor_mul(sq.t[:, ts(pc, 256)], of32.t[:, oc], of32.t[:, oc]), "dve")
                        p.post(t, reads=[fsf, flf, dfm, vf, gbh], writes=[of32, tmx[0], sq])
                    SEC("dve", mmx)

                if upto == "m2":
                    continue
                def mrm(t, h=h):
                    p.pre(t, "pe", reads=[sq, colsel], writes=[PC])
                    for kci in range(2):
                        pc = 2 * h + kci
                        inst = t.matmul(PC.t[32 * h:32 * h + 32, 0:256], colsel.t[:, 32:64],
                                        sq.t[:, ts(pc, 256)], start=(kci == 0), stop=(kci == 1),
                                        tile_position=(0, 32 * h))
                    p.post(p.fin(inst, "pe"), reads=[sq, colsel], writes=[PC])
                SEC("pe", mrm)

                if upto == "m3":
                    continue
                def mrs(a, h=h):
                    p.pre(a, "act", reads=[PC], writes=[nrm])
                    p.fin(a.activation(nrm.t[32 * h:32 * h + 32, 0:256], PC.t[32 * h:32 * h + 32, 0:256],
                                       AF.Ln, bias=epsv.t[32 * h:32 * h + 32, 0:1]), "act")
                    t = p.fin(a.activation(nrm.t[32 * h:32 * h + 32, 0:256], nrm.t[32 * h:32 * h + 32, 0:256],
                                           AF.Exp, scale=epsv.t[32 * h:32 * h + 32, 2:3]), "act")
                    p.post(t, reads=[PC], writes=[nrm])
                SEC("act", mrs)

                if upto == "m4":
                    continue
                def mob(t, h=h):
                    p.pre(t, "pe", reads=[nrm, rself], writes=[PD])
                    inst = t.matmul(PD.t[:, 0:256], rself.t[:, ts(h, 128)],
                                    nrm.t[:, 0:256], start=True, stop=True)
                    p.post(p.fin(inst, "pe"), reads=[nrm, rself], writes=[PD])
                SEC("pe", mob)

                if upto == "m5":
                    continue
                def moa(v, h=h):
                    p.pre(v, "dve", reads=[PD, of32, onw], writes=[off])
                    for kci in range(2):
                        pc = 2 * h + kci
                        t = p.fin(v.scalar_tensor_tensor(off.t[:, ts(pc, 256)],
                                                         of32.t[:, kci * 256:(kci + 1) * 256],
                                                         onw.t[:, kci:kci + 1], PD.t[:, 0:256],
                                                         op0=OP.mult, op1=OP.mult), "dve")
                    p.post(t, reads=[PD, of32, onw], writes=[off])
                SEC("dve", moa)
            if "o" in TAPS:
                def otap(g):
                    for pc in range(8):
                        DMA(g, "qo", None, TAPS["o"][ts(pc, 128), sco], off, off.t[:, ts(pc, 256)])
                SEC("pool", otap)

            if upto == "mix":
                return
            for tsl in range(2):
                b = 0

                def wx(g, tsl=tsl, b=b):
                    DMA(g, "qxr", xres[0], xres[0].t[:, :],
                        None, x_h[HALO + sc * SCT + tsl * 128: HALO + sc * SCT + (tsl + 1) * 128, :])
                SEC("pool", wx)
                if tsl == 0:
                    PQ = [PA, PBp, PC, PD]
                    for kc in range(8):
                        wb = wob[kc % 2]

                        def wld(g, kc=kc, wb=wb):
                            DMA(g, ("qb0" if kc % 2 == 0 else "qb1"), wb, wb.t[:, :],
                                None, w_h["wo"][ts(kc, 128), :])
                        SEC("pool", wld)

                        def wmm(t, kc=kc, wb=wb):
                            p.pre(t, "pe", reads=[off, wb], writes=PQ if kc == 0 else [])
                            for tsl2 in range(2):
                                for nch in range(2):
                                    inst = t.matmul(PQ[tsl2 * 2 + nch].t[:, 0:512],
                                                    off.t[:, kc * 256 + tsl2 * 128: kc * 256 + (tsl2 + 1) * 128],
                                                    wb.t[:, ts(nch, 512)],
                                                    start=(kc == 0), stop=(kc == 7))
                            tk = p.fin(inst, "pe")
                            p.post(tk, reads=[off, wb], writes=PQ if kc == 7 else [])
                        SEC("pe", wmm)
                PAx, PBx = (PA, PBp) if tsl == 0 else (PC, PD)

                def wlnX(v, tsl=tsl, PAx=PAx, PBx=PBx):
                    p.pre(v, "dve", reads=[PAx, PBx], writes=[mv])
                    p.fin(v.bn_stats(mv.t[:, 0:6], PAx.t[:, 0:512]), "dve")
                    p.fin(v.bn_stats(mv.t[:, 6:12], PBx.t[:, 0:512]), "dve")
                    t = p.fin(v.bn_aggr(mv.t[:, 12:14], mv.t[:, 0:12]), "dve")
                    p.post(t, reads=[PAx, PBx], writes=[mv])
                SEC("dve", wlnX)

                def wrs(a):
                    p.pre(a, "act", reads=[mv], writes=[mv])
                    p.fin(a.activation(mv.t[:, 14:15], mv.t[:, 13:14], AF.Ln, bias=epsv.t[:, 0:1]), "act")
                    t = p.fin(a.activation(mv.t[:, 15:16], mv.t[:, 14:15], AF.Exp, scale=epsv.t[:, 2:3]), "act")
                    p.post(t, writes=[mv])
                SEC("act", wrs)

                def wh1(v, tsl=tsl, PAx=PAx, PBx=PBx):
                    p.pre(v, "dve", reads=[PAx, PBx, mv, xres[0]], writes=[sc1, h1t[0]])
                    p.fin(v.tensor_scalar(sc1.t[:, 0:512], PAx.t[:, 0:512], mv.t[:, 12:13],
                                          mv.t[:, 15:16], op0=OP.subtract, op1=OP.mult), "dve")
                    p.fin(v.tensor_scalar(sc1.t[:, 512:1024], PBx.t[:, 0:512], mv.t[:, 12:13],
                                          mv.t[:, 15:16], op0=OP.subtract, op1=OP.mult), "dve")
                    t = p.fin(v.tensor_add(h1t[0].t[:, :], sc1.t[:, :], xres[0].t[:, :]), "dve")
                    p.post(t, reads=[PAx, PBx, mv, xres[0]], writes=[sc1, h1t[0]])
                    p.pre(v, "dve", reads=[h1t[0]], writes=[mv])
                    p.fin(v.bn_stats(mv.t[:, 0:6], h1t[0].t[:, 0:512]), "dve")
                    p.fin(v.bn_stats(mv.t[:, 6:12], h1t[0].t[:, 512:1024]), "dve")
                    t = p.fin(v.bn_aggr(mv.t[:, 12:14], mv.t[:, 0:12]), "dve")
                    p.post(t, reads=[h1t[0]], writes=[mv])
                SEC("dve", wh1)

                def wr4(a):
                    p.pre(a, "act", reads=[mv], writes=[mv])
                    p.fin(a.activation(mv.t[:, 14:15], mv.t[:, 13:14], AF.Ln, bias=epsv.t[:, 0:1]), "act")
                    t = p.fin(a.activation(mv.t[:, 15:16], mv.t[:, 14:15], AF.Exp, scale=epsv.t[:, 2:3]), "act")
                    p.post(t, writes=[mv])
                SEC("act", wr4)

                def wxi(v, tsl=tsl):
                    p.pre(v, "dve", reads=[h1t[0], mv], writes=[xint[0]])
                    t = p.fin(v.tensor_scalar(xint[0].t[:, :], h1t[0].t[:, :], mv.t[:, 12:13],
                                              mv.t[:, 15:16], op0=OP.subtract, op1=OP.mult), "dve")
                    p.post(t, reads=[h1t[0], mv], writes=[xint[0]])
                SEC("dve", wxi)

                def wst(g, tsl=tsl):
                    rows = slice(sc * SCT + tsl * 128, sc * SCT + (tsl + 1) * 128)
                    DMA(g, "qhx", Dh1, scr_h1[rows, :], h1t[0], h1t[0].t[:, :])
                    DMA(g, "qhx", Dxin, scr_xin[rows, :], xint[0], xint[0].t[:, :])

                    if "h1" in TAPS:
                        DMA(g, "qo", None, TAPS["h1"][rows, :], h1t[0], h1t[0].t[:, :])
                    if "xin" in TAPS:
                        DMA(g, "qo", None, TAPS["xin"][rows, :], xint[0], xint[0].t[:, :])
                SEC("pool", wst)
                p.endburst("qhx", Dh1, Dxin, h1t[0], xint[0])

        fine = None
        if stop_after and stop_after.startswith("f_"):
            fine = stop_after[2:]
        nsc1 = 1 if fine else NSC
        for sc in range(nsc1):
            emit_sc(sc, False, upto=fine)
            if fine not in ("proj", "beta", "norm", "est"):
                emit_scan(sc, False, upto=fine)
        if fine:
            early_out()
            return nc

        if stop_after == "pass1":
            early_out()
            return nc

        def ccpre(v):
            p.pre(v, "dve", reads=[hflagn], writes=[sfp])
            t = p.fin(v.tensor_scalar_mul(sfp.t[:, :], sfp.t[:, :], hflagn.t[:, 0:1]), "dve")
            p.post(t, reads=[hflagn], writes=[sfp])
        SEC("dve", ccpre)

        def ccw(g):
            p.pre(g, "pool", reads=[sfp], writes=[Dci])
            tok = p.dmatok(g.dma_start(out=cc_i.rearrange("(b q) d -> q b d", q=128)[:, :, :],
                                       in_=sfp.t.rearrange("q (b d) -> q b d", b=8)[:, :, :]), "qg")
            p.post(tok, reads=[sfp], writes=[Dci])
        SEC("pool", ccw)

        def ccr(g):
            p.pre(g, "pool", reads=[Dci], writes=[Dco])
            inst = g.collective_compute("AllReduce", OP.add,
                                        replica_groups=[[2 * i, 2 * i + 1] for i in range(pairs)],
                                        ins=[cc_i.ap()], outs=[cc_o.ap()])
            inst.then_inc(p.sem["cc"], 1)
            p.cnt["cc"] += 1
            p.post(("cc", p.cnt["cc"]), reads=[Dci], writes=[Dco])
        SEC("pool", ccr)

        def ccl(g):
            p.pre(g, "pool", reads=[Dco], writes=[sfp])
            tok = p.dmatok(g.dma_start(out=sfp.t.rearrange("q (b d) -> q b d", b=8)[:, :, :],
                                       in_=cc_o.rearrange("(b q) d -> q b d", q=128)[:, :, :]), "qg")
            p.post(tok, reads=[Dco], writes=[sfp])
        SEC("pool", ccl)

        def ccm(v):
            p.pre(v, "dve", reads=[sfp, hflag], writes=[sfp, sbfb])
            p.fin(v.tensor_scalar_mul(sfp.t[:, :], sfp.t[:, :], hflag.t[:, 0:1]), "dve")
            t = p.fin(v.tensor_copy(sbfb.t[:, :], sfp.t[:, :]), "dve")
            p.post(t, reads=[hflag], writes=[sfp, sbfb])
            if "send" in TAPS:
                pass
        SEC("dve", ccm)
        if "send" in TAPS:
            def sstap(g):
                DMA(g, "qo", None, TAPS["send"][:, :], sfp, sfp.t[:, :])
            SEC("pool", sstap)

        if stop_after == "cc":
            early_out()
            return nc

        tfine = stop_after[2:] if (stop_after and stop_after.startswith("t_")) else None
        for sc in range(NSC):
            emit_sc(sc, True)
            emit_scan(sc, True)
            if stop_after != "pass2scan":
                emit_tail(sc, upto=tfine)
        if stop_after in ("pass2scan", "tail") or tfine:
            early_out()
            return nc

        ats.close()

        snap = dict(p.cnt)

        def _fence(e, ek):
            for s_, v_ in snap.items():
                if v_ > 0:
                    p._w(e, ek, (s_, v_))
        for ek_ in ("pe", "dve", "act", "pool", "sp"):
            SEC(ek_, lambda e, ek_=ek_: _fence(e, ek_))

        xing = sb("xing_s", [128, 4 * D])
        xf = sb("xf_s", [128, 8 * 512])
        w1c = [sb(f"w1c{i}", [128, 8 * 512]) for i in range(2)]
        w2c = [sb(f"w2c{i}", [128, 4 * D]) for i in range(2)]
        hgrp = sb("hgrp_s", [128, 32 * 512])
        affb = sb("aff_s", [128, 8], F32)
        smax = sb("smax_s", [128, 8], F32)
        sidx = Bf(es.enter_context(nc.sbuf_tensor("sidx_s", [128, 8], U32)))
        eidf = sb("eidf_s", [128, 4], F32)
        alphag = sb("alphag_s", [128, 4], F32)
        maskg = sb("maskg_s", [128, 32], F32)
        gtok = sb("gtok_s", [128, D], F32)
        btok = sb("btok_s", [128, D], F32)
        accg = sb("accg_s", [128, 4 * D], F32)
        of32f = sb("of32f_s", [128, D], F32)
        cens = sb("cens_s", [128, 64])
        rselF = sb("rselF_s", [128, 512])
        b1r = sb("b1r_s", [128, 256], F32)
        b2p = sb("b2p_s", [128, 2 * D])
        xlt = [sb(f"xlt{i}", [128, D]) for i in range(2)]

        def ffn_ld(g):
            ld = lambda o_ap, i_ap: p.dmatok(g.dma_start(out=o_ap, in_=i_ap), "qw")
            for kc in range(8):
                ld(cens.t[:, ts(kc, 8)], cen_h[ts(kc, 128), :])
            ld(b1r.t[:, :], b1_h[:, :])
            ld(b2p.t[:, 0:D], b2p_h[0:128, :]); ld(b2p.t[:, D:2 * D], b2p_h[128:256, :])
            ld(rselF.t[:, :], rsel_h[:, :])
            tok = ("qw", p.cnt["qw"])
            for b in (cens, b1r, b2p, rselF):
                b.w = tok
        SEC("pool", ffn_ld)

        for g_ in range(NG):
            grows = slice(g_ * 512, (g_ + 1) * 512)

            def n_xload(s, g_=g_):
                for tl in range(4):
                    DMA(s, "qx", xing, xing.t[:, ts(tl, D)],
                        Dxin, scr_xin[g_ * 512 + tl * 128: g_ * 512 + (tl + 1) * 128, :], ek="sp")
            SEC("sp", n_xload)
            p.endburst("qx", Dxin)
            xing.w = ("qx", p.cnt["qx"]); xing.rs = []

            def n_xhop(s, g_=g_):
                for pc in range(8):
                    TDMA(s, "hxf", xf, xf.t[:, ts(pc, 512)], Dxin, scr_xin[grows, ts(pc, 128)])
            SEC("sp", n_xhop)
            p.endburst("hxf", Dxin)
            xf.w = ("hxf", p.cnt["hxf"]); xf.rs = []

            for tl in range(4):
                def n_aff(t, tl=tl):
                    p.pre(t, "pe", reads=[xf, cens], writes=[PC])
                    for kc in range(8):
                        inst = t.matmul(PC.t[:, 0:8], xf.t[:, kc * 512 + tl * 128: kc * 512 + (tl + 1) * 128],
                                        cens.t[:, ts(kc, 8)], start=(kc == 0), stop=(kc == 7))
                    p.post(p.fin(inst, "pe"), reads=[xf, cens], writes=[PC])
                SEC("pe", n_aff)

                def n_arg(v, tl=tl):
                    p.pre(v, "dve", reads=[PC], writes=[affb, smax, sidx, eidf, maskg])
                    p.fin(v.tensor_copy(affb.t[:, :], PC.t[:, 0:8]), "dve")
                    p.fin(v.max(smax.t[:, :], affb.t[:, :]), "dve")
                    p.fin(v.max_index(sidx.t[:, :], smax.t[:, :], affb.t[:, :]), "dve")
                    p.fin(v.tensor_copy(eidf.t[:, tl:tl + 1], sidx.t[:, 0:1]), "dve")
                    for e in range(E):
                        t = p.fin(v.tensor_single_scalar(maskg.t[:, tl * 8 + e: tl * 8 + e + 1],
                                                         eidf.t[:, tl:tl + 1], float(e), OP.is_equal), "dve")
                    p.post(t, reads=[PC], writes=[affb, smax, sidx, eidf, maskg])
                SEC("dve", n_arg)

                def n_alp(a, tl=tl):
                    p.pre(a, "act", reads=[smax], writes=[alphag])
                    t = p.fin(a.activation(alphag.t[:, tl:tl + 1], smax.t[:, 0:1], AF.Sigmoid), "act")
                    p.post(t, reads=[smax], writes=[alphag])
                SEC("act", n_alp)

                def n_gth(g, tl=tl):
                    p.pre(g, "pool", reads=[sidx], writes=[gtok, btok])
                    tok = p.dmatok(g.indirect_dma_start(
                        out=gtok.t[:, :], out_offset=None, in_=lng_h[:, :],
                        in_offset=bass.IndirectOffsetOnAxis(sidx.t[:, 0:1], 0)), "qg")
                    tok = p.dmatok(g.indirect_dma_start(
                        out=btok.t[:, :], out_offset=None, in_=lnb_h[:, :],
                        in_offset=bass.IndirectOffsetOnAxis(sidx.t[:, 0:1], 0)), "qg")
                    p.post(tok, reads=[sidx], writes=[gtok, btok])
                SEC("pool", n_gth)

                def n_ln(v, tl=tl):
                    p.pre(v, "dve", reads=[xing], writes=[mv])
                    p.fin(v.bn_stats(mv.t[:, 0:6], xing.t[:, tl * D: tl * D + 512]), "dve")
                    p.fin(v.bn_stats(mv.t[:, 6:12], xing.t[:, tl * D + 512: (tl + 1) * D]), "dve")
                    t = p.fin(v.bn_aggr(mv.t[:, 12:14], mv.t[:, 0:12]), "dve")
                    p.post(t, reads=[xing], writes=[mv])
                SEC("dve", n_ln)

                def n_lr(a):
                    p.pre(a, "act", reads=[mv], writes=[mv])
                    p.fin(a.activation(mv.t[:, 14:15], mv.t[:, 13:14], AF.Ln, bias=epsv.t[:, 0:1]), "act")
                    t = p.fin(a.activation(mv.t[:, 15:16], mv.t[:, 14:15], AF.Exp, scale=epsv.t[:, 2:3]), "act")
                    p.post(t, writes=[mv])
                SEC("act", n_lr)

                def n_xln(v, tl=tl):
                    b = tl % 2
                    p.pre(v, "dve", reads=[xing, mv, gtok, btok], writes=[sc1, xlt[b]])
                    p.fin(v.tensor_scalar(sc1.t[:, :], xing.t[:, ts(tl, D)], mv.t[:, 12:13],
                                          mv.t[:, 15:16], op0=OP.subtract, op1=OP.mult), "dve")
                    p.fin(v.tensor_mul(sc1.t[:, :], sc1.t[:, :], gtok.t[:, :]), "dve")
                    t = p.fin(v.tensor_add(xlt[b].t[:, :], sc1.t[:, :], btok.t[:, :]), "dve")
                    p.post(t, reads=[xing, mv, gtok, btok], writes=[sc1, xlt[b]])
                SEC("dve", n_xln)

                def n_xw(g, tl=tl, g_=g_):
                    b = tl % 2
                    DMA(g, ("qhx" if tl % 2 == 0 else "qhxb"), Dxln,
                        scr_xln[g_ * 512 + tl * 128: g_ * 512 + (tl + 1) * 128, :],
                        xlt[b], xlt[b].t[:, :])
                    if "xln" in TAPS:
                        DMA(g, "qo", None, TAPS["xln"][g_ * 512 + tl * 128: g_ * 512 + (tl + 1) * 128, :],
                            xlt[b], xlt[b].t[:, :])
                    if "eid" in TAPS:
                        DMA(g, "qo", None, TAPS["eid"][g_ * 4 + tl, :, 0:1], eidf, eidf.t[:, tl:tl + 1],
                            allow_slow_non_contiguous=True)
                        DMA(g, "qo", None, TAPS["eid"][g_ * 4 + tl, :, 1:2], alphag, alphag.t[:, tl:tl + 1],
                            allow_slow_non_contiguous=True)
                SEC("pool", n_xw)

            def n_lhop(s, g_=g_):
                for pc in range(8):
                    TDMA(s, "hxf", xf, xf.t[:, ts(pc, 512)], Dxln, scr_xln[grows, ts(pc, 128)])
            SEC("sp", n_lhop)
            p.endburst("hxf", Dxln)
            xf.w = ("hxf", p.cnt["hxf"]); xf.rs = []

            def n_acz(v):
                p.pre(v, "dve", writes=[accg])
                t = p.fin(v.memset(accg.t[:, :], 0.0), "dve")
                p.post(t, writes=[accg])
            SEC("dve", n_acz)

            for e in range(E):
                for fc in range(8):
                    wb = w1c[fc % 2]

                    def n_w1(g, e=e, fc=fc, wb=wb):
                        for kc in range(8):
                            DMA(g, ("qA" if fc % 2 == 0 else "qB"), wb, wb.t[:, ts(kc, 512)],
                                Dw1, scr_w1[e, ts(kc, 128), ts(fc, 512)])
                        wb.w = (("qA" if fc % 2 == 0 else "qB"), p.cnt["qA" if fc % 2 == 0 else "qB"]); wb.rs = []
                    SEC("pool", n_w1)

                    for ms in range(4):
                        PT = [PA, PBp][ms % 2]

                        def n_f1(t, fc=fc, ms=ms, wb=wb, PT=PT):
                            p.pre(t, "pe", reads=[wb, xf], writes=[PT])
                            for kc in range(8):
                                inst = t.matmul(PT.t[:, 0:512],
                                                wb.t[:, kc * 512 + ms * 128: kc * 512 + (ms + 1) * 128],
                                                xf.t[:, ts(kc, 512)], start=(kc == 0), stop=(kc == 7))
                            p.post(p.fin(inst, "pe"), reads=[wb, xf], writes=[PT])
                        SEC("pe", n_f1)

                        def n_r1(a, e=e, fc=fc, ms=ms, PT=PT):
                            p.pre(a, "act", reads=[PT, b1r], writes=[hgrp])
                            t = p.fin(a.activation(hgrp.t[:, (fc * 4 + ms) * 512: (fc * 4 + ms + 1) * 512],
                                                   PT.t[:, 0:512], AF.Relu,
                                                   bias=b1r.t[:, e * 32 + fc * 4 + ms: e * 32 + fc * 4 + ms + 1]), "act")
                            p.post(t, reads=[PT, b1r], writes=[hgrp])
                        SEC("act", n_r1)

                PT2 = [[(PG1, slice(0, 512)), (PG1, slice(512, 1024))],
                       [(PG2, slice(0, 512)), (PG2, slice(512, 1024))],
                       [(PC, slice(0, 512)), (PD, slice(0, 512))],
                       [(PA, slice(0, 512)), (PBp, slice(0, 512))]]
                for fc in range(8):
                    wb = w2c[fc % 2]

                    def n_w2(g, e=e, fc=fc, wb=wb):
                        for kc4 in range(4):
                            DMA(g, ("qC" if fc % 2 == 0 else "qD"), wb, wb.t[:, ts(kc4, D)],
                                Dw2, scr_w2[e, fc * 512 + kc4 * 128: fc * 512 + (kc4 + 1) * 128, :])
                        wb.w = (("qC" if fc % 2 == 0 else "qD"), p.cnt["qC" if fc % 2 == 0 else "qD"]); wb.rs = []
                    SEC("pool", n_w2)

                    for tl in range(4):
                        def n_f2(t, e=e, fc=fc, tl=tl, wb=wb):
                            pts = PT2[tl]
                            rd = [wb, hgrp, rselF, b2p]
                            p.pre(t, "pe", reads=rd, writes=[pts[0][0], pts[1][0]])
                            for nch, (PT, reg) in enumerate(pts):
                                if fc == 0:
                                    be = 32 * (e % 4)
                                    bc = (e // 4) * D
                                    t.matmul(PT.t[:, reg], rselF.t[:, ts(e % 4, 128)],
                                             b2p.t[:, bc + nch * 512: bc + (nch + 1) * 512],
                                             start=True, stop=False)
                                for kc4 in range(4):
                                    inst = t.matmul(PT.t[:, reg],
                                                    hgrp.t[:, (fc * 4 + kc4) * 512 + tl * 128: (fc * 4 + kc4) * 512 + (tl + 1) * 128],
                                                    wb.t[:, kc4 * D + nch * 512: kc4 * D + (nch + 1) * 512],
                                                    start=False, stop=(fc == 7 and kc4 == 3))
                                tk = p.fin(inst, "pe")
                                p.post(tk, reads=rd, writes=[PT])
                        SEC("pe", n_f2)

                for tl in range(4):
                    def n_acc(v, e=e, tl=tl):
                        pts = PT2[tl]
                        p.pre(v, "dve", reads=[pts[0][0], pts[1][0], maskg], writes=[accg])
                        for nch, (PT, reg) in enumerate(pts):
                            t = p.fin(v.scalar_tensor_tensor(
                                accg.t[:, tl * D + nch * 512: tl * D + (nch + 1) * 512],
                                PT.t[:, reg], maskg.t[:, tl * 8 + e: tl * 8 + e + 1],
                                accg.t[:, tl * D + nch * 512: tl * D + (nch + 1) * 512],
                                op0=OP.mult, op1=OP.add), "dve")
                        p.post(t, reads=[pts[0][0], pts[1][0], maskg], writes=[accg])
                    SEC("dve", n_acc)

            for tl in range(4):
                b = tl % 2

                def n_h1l(s, g_=g_, tl=tl, b=b):
                    DMA(s, "qx", h1t[0], h1t[0].t[:, :],
                        Dh1, scr_h1[g_ * 512 + tl * 128: g_ * 512 + (tl + 1) * 128, :], ek="sp")
                SEC("sp", n_h1l)

                def n_fbo(v, tl=tl, b=b):
                    p.pre(v, "dve", reads=[accg, alphag, xing], writes=[of32f, mv])
                    p.fin(v.scalar_tensor_tensor(of32f.t[:, 0:1024], accg.t[:, ts(tl, D)],
                                                 alphag.t[:, tl:tl + 1], xing.t[:, ts(tl, D)],
                                                 op0=OP.mult, op1=OP.add), "dve")
                    p.fin(v.bn_stats(mv.t[:, 0:6], of32f.t[:, 0:512]), "dve")
                    p.fin(v.bn_stats(mv.t[:, 6:12], of32f.t[:, 512:1024]), "dve")
                    t = p.fin(v.bn_aggr(mv.t[:, 12:14], mv.t[:, 0:12]), "dve")
                    p.post(t, reads=[accg, alphag, xing], writes=[of32f, mv])
                SEC("dve", n_fbo)

                def n_fr(a):
                    p.pre(a, "act", reads=[mv], writes=[mv])
                    p.fin(a.activation(mv.t[:, 14:15], mv.t[:, 13:14], AF.Ln, bias=epsv.t[:, 0:1]), "act")
                    t = p.fin(a.activation(mv.t[:, 15:16], mv.t[:, 14:15], AF.Exp, scale=epsv.t[:, 2:3]), "act")
                    p.post(t, writes=[mv])
                SEC("act", n_fr)

                def n_out(v, tl=tl, b=b):
                    p.pre(v, "dve", reads=[of32f, mv, h1t[0]], writes=[sc1])
                    p.fin(v.tensor_scalar(sc1.t[:, :], of32f.t[:, 0:1024], mv.t[:, 12:13],
                                          mv.t[:, 15:16], op0=OP.subtract, op1=OP.mult), "dve")
                    t = p.fin(v.tensor_add(sc1.t[:, :], sc1.t[:, :], h1t[0].t[:, :]), "dve")
                    p.post(t, reads=[of32f, mv, h1t[0]], writes=[sc1])
                SEC("dve", n_out)

                def n_odma(g, g_=g_, tl=tl):
                    DMA(g, "qo", None, out_h[g_ * 512 + tl * 128: g_ * 512 + (tl + 1) * 128, :],
                        sc1, sc1.t[:, :])
                SEC("pool", n_odma)

        def fino(g):
            g.wait_ge(p.sem["qo"], p.cnt["qo"])
            g.wait_ge(p.sem["qg"], p.cnt["qg"])
        SEC("pool", fino)

    return nc



def _consts(NSC):
    NT = NSC * SCT
    i32t8 = np.zeros((128, 256), np.float32)
    for c in range(256):
        i32t8[(c % 32)::32, c] = 0.0
    pr = np.arange(128)
    cc = np.arange(256)
    i32t8 = (pr[:, None] % 32 == cc[None, :] % 32).astype(np.float32)
    oneskm = np.ones((128, 128), np.float32)
    m1km = -np.ones((128, 128), np.float32)
    colsel = np.zeros((128, 64), np.float32)
    colsel[:, 0] = 1.0
    colsel[:, 32] = 1.0 / 256.0
    statsel = np.zeros((128, 288), np.float32)
    for blkk in range(3):
        for pc in range(8):
            statsel[:, blkk * 96 + pc * 12 + blkk * 4 + pc // 2] = 1.0 / 256.0
    g2m = np.zeros((128, CPS * 256), np.float32)
    for m in range(CPS):
        for c in range(256):
            tchunk, tt_ = c // 32, c % 32
            if tchunk > m:
                g2m[:, m * 256 + c] = 1.0
            elif tchunk == m:
                g2m[pr % 32 <= tt_, m * 256 + c] = 1.0
    nmask = np.zeros((128, 256), np.float32)
    for c in range(256):
        nmask[pr % 32 < (c % 32), c] = 1.0
    i32sp = np.zeros((128, 256), np.float32)
    for n in range(8):
        for c in range(32):
            i32sp[32 * (n % 4) + c, 32 * n + c] = 1.0
    gatesel = np.zeros((128, 12 * 128), np.float32)
    for r in range(12):
        gatesel[r, r * 128:(r + 1) * 128] = 1.0
    rsel = np.zeros((128, 512), np.float32)
    rselm = np.zeros((128, 512), np.float32)
    for h in range(4):
        rsel[32 * h, h * 128:(h + 1) * 128] = 1.0
        rselm[32 * h, h * 128:(h + 1) * 128] = -1.0
    return dict(i32t8=i32t8, i32sp=i32sp, oneskm=oneskm, m1km=m1km, colsel=colsel,
                statsel=statsel, g2m=g2m, nmask=nmask, gatesel=gatesel, rsel=rsel, rselm=rselm,
                rself=rsel.copy())


def make_inputs(core_inputs, NSC=8):
    gi = core_inputs
    NT = NSC * SCT
    B = gi["L0_attn_hidden_states"].shape[0]
    L = gi["L0_attn_hidden_states"].shape[1]
    halves = L // NT
    n_cores = B * halves
    c = _consts(NSC)
    wb = gi["L0_attn_Wb"]; wid = gi["L0_attn_Wid"]
    wbid = np.zeros((D, 256), np.float32)
    for h in range(4):
        wbid[:, 32 * h] = wb[:, h]
        wbid[:, 128 + 32 * h] = wid[:, h]
    bidp = np.zeros((128, 1), np.float32)
    alphap = np.zeros((128, 1), np.float32)
    for h in range(4):
        bidp[32 * h, 0] = gi["L0_attn_bid"][h]
        alphap[32 * h, 0] = gi["L0_attn_alpha_id"][h]
    conv = {}
    for nn, key in (("q", "L0_attn_conv_q"), ("k", "L0_attn_conv_k"), ("v", "L0_attn_conv_v")):
        conv[nn] = np.ascontiguousarray(
            gi[key].reshape(8, 128, 4).transpose(1, 0, 2).reshape(128, 32))
    firs = np.ascontiguousarray(
        gi["L0_attn_fir_short"].reshape(1024, 3).reshape(8, 128, 3).transpose(1, 0, 2).reshape(128, 24))
    firl = np.ascontiguousarray(
        gi["L0_attn_fir_long"].reshape(1024, 31).reshape(8, 128, 31).transpose(1, 0, 2).reshape(128, 248))
    wr1 = gi["L0_attn_Wr1"]
    mean_rows = [1024 + 8 * t + h for t in range(3) for h in range(4)]
    std_rows = [1024 + 8 * t + 4 + h for t in range(3) for h in range(4)]
    wr1m = np.ascontiguousarray(wr1[0:1024])
    wr1sm = np.ascontiguousarray(wr1[mean_rows])
    wr1ss = np.ascontiguousarray(wr1[std_rows])
    br1r = np.ascontiguousarray(gi["L0_attn_br1"].reshape(16, 128).T)
    br2r = gi["L0_attn_br2"].reshape(12, 1)
    ltg12 = np.repeat(gi["L0_attn_log_tau_group"], 6).reshape(12, 1)
    onwr = np.ascontiguousarray(gi["L0_attn_o_norm_w"].reshape(2, 128).T)
    cent = np.ascontiguousarray(gi["L0_ffn_centroids"].T)
    b1r = np.zeros((128, E * 32), np.float32)
    for e in range(E):
        b1r[:, e * 32:(e + 1) * 32] = gi["L0_ffn_b1"][e].reshape(32, 128).T
    b2p = np.zeros((E * 32, D), np.float32)
    for e in range(E):
        b2p[32 * e] = gi["L0_ffn_b2"][e]

    common = dict(
        wq=gi["L0_attn_Wq"], wk=gi["L0_attn_Wk"], wv=gi["L0_attn_Wv"],
        wo=gi["L0_attn_Wo"], wbid=wbid, bidp=bidp, alphap=alphap,
        convq=conv["q"], convk=conv["k"], convv=conv["v"], firs=firs, firl=firl,
        wr1m=wr1m, wr1sm=wr1sm, wr1ss=wr1ss, br1r=br1r, wr2=gi["L0_attn_Wr2"],
        br2r=br2r, ltg12=ltg12, onwr=onwr,
        cent=cent, lng=gi["L0_ffn_ln_g"], lnb=gi["L0_ffn_ln_b"],
        w1=gi["L0_ffn_w1"], w2=gi["L0_ffn_w2"], b1r=b1r, b2p=b2p, **c)
    common["epsv"] = np.tile(np.array([1e-5, 1e-6, -0.5, 0.0], np.float32), (128, 1))
    common = {k: np.ascontiguousarray(v, dtype=np.float32) for k, v in common.items()}

    in_maps = []
    x = gi["L0_attn_hidden_states"]
    for core in range(n_cores):
        b, hf = core // halves, core % halves
        t0 = hf * NT
        if t0 == 0:
            xslice = np.concatenate([np.zeros((HALO, D), np.float32), x[b, 0:NT]], axis=0)
        else:
            xslice = x[b, t0 - HALO: t0 + NT]
        m = dict(common)
        m["x"] = np.ascontiguousarray(xslice, dtype=np.float32)
        m["hflag"] = np.full((128, 1), float(hf > 0), np.float32)
        m["hflagn"] = np.full((128, 1), float(hf == 0), np.float32)
        in_maps.append(m)
    return in_maps, n_cores


def kernel(**inputs):
    NSC = 8
    NT = NSC * SCT
    in_maps, n_cores = make_inputs(inputs, NSC)
    nc = build_program(NSC=NSC, pairs=n_cores // 2)
    res = run_bass_kernel_spmd(nc, in_maps, core_ids=list(range(n_cores)))
    B = inputs["L0_attn_hidden_states"].shape[0]
    L = inputs["L0_attn_hidden_states"].shape[1]
    halves = L // NT
    out = np.zeros((B, L, D), np.float32)
    for core in range(n_cores):
        b, hf = core // halves, core % halves
        out[b, hf * NT:(hf + 1) * NT] = np.asarray(res.results[core]["out"]).reshape(NT, D)
    return out
